# version 03 (final)
# baseline (speedup 1.0000x reference)
import numpy as np
import ml_dtypes
import concourse.bass as bass
import concourse.mybir as mybir
from concourse.bass_utils import run_bass_kernel_spmd

BF = ml_dtypes.bfloat16
F32 = mybir.dt.float32
BF16 = mybir.dt.bfloat16
I32 = mybir.dt.int32
U8 = mybir.dt.uint8
AF = mybir.ActivationFunctionType
ALU = mybir.AluOpType
AX = mybir.AxisListType

B, T, C, H, N = 2, 4096, 1024, 16, 64
L = 64
NCH = T // L
TG = 128
NCK = 8
TT = 512
NT = T // TT

DEBUG_TAPS = False
BUILD_STAGES = 99
WKV_PAIRS = None
WKV_CGS = None
WKV_PH = 99


class Sem:
    __slots__ = ("h", "n", "name")

    def __init__(self, h, name):
        self.h, self.n, self.name = h, 0, name

    def inc(self, k=1):
        self.n += k
        return self.n


class EQ:

    def __init__(self, q, guard_sem=None):
        self.q = q
        self.gs = guard_sem
        self.flush = None
        self.last = None

    def wait(self, sem, val):
        if val > sem.n:
            raise RuntimeError(f"wait {sem.name} {val} > planned {sem.n}")
        self.q.append(("w", sem.h, val))

    def _pe_guard(self, meth, args):
        if self.gs is None or meth not in ("matmul", "transpose"):
            if meth not in ("nop",):
                self.last = None if meth not in ("matmul", "transpose") else self.last
            return
        out, lhsT = args[0], args[1]
        try:
            r0 = lhsT.base_partition()
            r1 = r0 + lhsT.partition_size()
            c0 = out.base_partition()
            c1 = c0 + out.partition_size()
        except Exception:
            self.last = None
            return
        rows, cols = (r0, r1), (c0, c1)
        if self.last is not None:
            lk, lr, lc = self.last
            rows_disjoint = rows[0] >= lr[1] or rows[1] <= lr[0]
            cols_intersect = not (cols[0] >= lc[1] or cols[1] <= lc[0])
            tr_pair = (lk == "transpose" and meth == "transpose" and cols_intersect)
            if (rows_disjoint and cols_intersect) or tr_pair:
                for i in range(len(self.q) - 1, -1, -1):
                    it = self.q[i]
                    if it[0] == "w" or (it[0] in ("i", "r")
                                        and it[1] not in ("matmul", "transpose", "nop")):
                        continue
                    if it[0] == "i":
                        self.q.append(("w", self.gs.h, self.gs.n))
                    else:
                        v = self.gs.inc(1)
                        self.q[i] = ("i", it[1], it[2], it[3], self.gs.h, 1)
                        self.q.append(("w", self.gs.h, v))
                    break
        self.last = (meth, rows, cols)

    def op(self, meth, *args, sem=None, inc=1, **kw):
        if meth == "engine_nop" and self.flush is not None:
            meth, args = "tensor_scalar_mul", (self.flush, self.flush, 1.0)
        if meth in ("tensor_tensor", "scalar_tensor_tensor", "tensor_tensor_scan"):
            npsum = sum(1 for a in args[1:]
                        if hasattr(a, "space")
                        and a.space == bass.MemorySpace.PSUM)
            if npsum > 1:
                raise RuntimeError(f"{meth}: {npsum} PSUM inputs (max 1)")
        self._pe_guard(meth, args)
        if sem is not None:
            v = sem.inc(inc)
            self.q.append(("i", meth, args, kw, sem.h, inc))
            return v
        self.q.append(("r", meth, args, kw))
        return None

    def dma(self, dst, src, sem, **kw):
        return self.op("dma_start", dst, src, sem=sem, inc=16, **kw)


class Prog:
    ENGINES = ("sync", "scalar", "vector", "tensor", "gpsimd")

    def __init__(self, nc):
        self.nc = nc
        self.queues = {e: [] for e in self.ENGINES}
        for e in self.ENGINES:
            setattr(self, e, EQ(self.queues[e]))

    def attach_guard(self, sem):
        self.tensor.gs = sem

    def finalize(self, blk):
        for eng in self.ENGINES:
            items = self.queues[eng]

            def runner(e, items=items):
                for it in items:
                    if it[0] == "w":
                        e.wait_ge(it[1], it[2])
                    elif it[0] == "r":
                        getattr(e, it[1])(*it[2], **it[3])
                    else:
                        getattr(e, it[1])(*it[2], **it[3]).then_inc(it[4], it[5])
            getattr(blk, eng)(runner)

    def n_inst(self):
        return sum(len(q) for q in self.queues.values())


class PsumPool:

    def __init__(self, nc, st):
        self.tiles = [st.enter_context(nc.psum_tensor(f"pp{i}", [128, 512], F32))
                      for i in range(8)]
        self.free_at = [None] * 8
        self.held = set()
        self.rr = 0

    def acquire(self, tq, hold=False):
        while True:
            i = self.rr
            self.rr = (self.rr + 1) % 8
            if i not in self.held:
                break
        if hold:
            self.held.add(i)
        if self.free_at[i] is not None:
            sem, val = self.free_at[i]
            tq.wait(sem, val)
            self.free_at[i] = None
        return i, self.tiles[i]

    def release(self, i, sem_val):
        self.free_at[i] = sem_val
        self.held.discard(i)


def build_program(dbg=DEBUG_TAPS, stages=BUILD_STAGES):
    import contextlib
    nc = bass.Bass("TRN2", target_bir_lowering=False, debug=False,
                   detect_race_conditions=False)
    st = contextlib.ExitStack()
    nc._exit_stack = st

    kind_s = "ExternalOutput" if dbg else "Internal"

    x_in = nc.dram_tensor("x", [B, T, C], F32, kind="ExternalInput").ap()
    out = nc.dram_tensor("out", [B, T, C], F32, kind="ExternalOutput").ap()

    shapes_attn = {
        "time_maa_x": [1, 1, C], "time_maa_rg": [1, 1, C], "time_maa_wa": [1, 1, C],
        "time_maa_k": [1, 1, C], "time_maa_v": [1, 1, C],
        "time_maa_w1": [C, 128], "time_maa_w2": [4, 32, C],
        "time_decay": [1, 1, C], "time_decay_w1": [C, 64], "time_decay_w2": [64, C],
        "time_faaaa": [1, 1, 16, 64], "time_aaaaa": [1, 1, C],
        "time_aaa_w1": [C, 16], "time_aaa_w2": [16, C],
        "time_kkk_w1": [C, 16], "time_kkk_w2": [16, C],
        "gate_w1": [C, 128], "gate_w2": [128, C],
        "ma_w1": [C, 16], "ma_w2": [16, C],
        "time_misc_a": [1, 1, C], "mk_w1": [C, 16], "mk_w2": [16, C],
        "time_misc_k": [1, 1, C],
        "w_receptance": [C, C], "w_key": [C, C], "w_value": [C, C], "w_output": [C, C],
        "ln_w": [C], "ln_b": [C],
    }
    W = {"attn_" + n: nc.dram_tensor("L0_attn_" + n, s, F32, kind="ExternalInput").ap()
         for n, s in shapes_attn.items()}
    shapes_ffn = {
        "wqkv": ("L0_ffn_wqkv", [2, C, 1536]), "q_scale": ("L0_ffn_q_scale", [2, 64]),
        "k_scale": ("L0_ffn_k_scale", [2, 64]), "wo": ("L0_ffn_wo", [2, 512, C]),
        "ln_g": ("L0_ffn_ln_g", [2, C]), "ln_b": ("L0_ffn_ln_b", [2, C]),
        "w1p": ("w1p", [2, C, 5632]), "b1p": ("b1p", [2, 5632]),
        "w2p": ("w2p", [2, 2816, C]), "b2": ("L0_ffn_b2", [2, C]),
        "rot_qc": ("rot_qc", [T, 64]), "rot_qs": ("rot_qs", [T, 64]),
        "rot_kc": ("rot_kc", [T, 64]), "rot_ks": ("rot_ks", [T, 64]),
    }
    for n, (tn, s) in shapes_ffn.items():
        W["ffn_" + n] = nc.dram_tensor(tn, s, F32, kind="ExternalInput").ap()

    SC = {}
    SC["xin"] = x_in
    SC["xbf"] = nc.dram_tensor("s_xbf", [B, T + TG, C], BF16, kind=kind_s).ap()
    for n in ("r", "k", "v", "a", "b", "g"):
        SC[n] = nc.dram_tensor("s_" + n, [B, C, T], BF16, kind=kind_s).ap()
    SC["w"] = nc.dram_tensor("s_w", [B, C, T], F32, kind=kind_s).ap()
    SC["y"] = nc.dram_tensor("s_y", [B, T, C], BF16, kind=kind_s).ap()
    SC["x1"] = nc.dram_tensor("s_x1", [B, T + TG, C], F32, kind=kind_s).ap()
    SC["x1b"] = nc.dram_tensor("s_x1b", [B, T + TG, C], BF16, kind=kind_s).ap()
    SC["x2"] = nc.dram_tensor("s_x2", [B, T + TG, C], F32, kind=kind_s).ap()
    SC["x2b"] = nc.dram_tensor("s_x2b", [B, T + TG, C], BF16, kind=kind_s).ap()
    SC["x3"] = nc.dram_tensor("s_x3", [B, T + TG, C], F32, kind=kind_s).ap()
    SC["x3b"] = nc.dram_tensor("s_x3b", [B, T + TG, C], BF16, kind=kind_s).ap()
    SC["lnb"] = nc.dram_tensor("s_lnb", [B, T, C], BF16, kind=kind_s).ap()
    SC["m"] = nc.dram_tensor("s_m", [B, 8, T], F32, kind=kind_s).ap()
    SC["wr_bf"] = nc.dram_tensor("h_wr", [C, C], BF16, kind="Internal").ap()
    SC["wk_bf"] = nc.dram_tensor("h_wk", [C, C], BF16, kind="Internal").ap()
    SC["wv_bf"] = nc.dram_tensor("h_wv", [C, C], BF16, kind="Internal").ap()
    SC["wo_bf"] = nc.dram_tensor("h_wo", [C, C], BF16, kind="Internal").ap()

    P = Prog(nc)
    S = {n: Sem(st.enter_context(nc.semaphore(n)), n)
         for n in ["ld", "ld2", "sw", "swp", "io", "mm", "dv", "ac", "stv", "bar", "dbg"]
         + [f"rs{i}" for i in range(16)]}
    pool = PsumPool(nc, st)
    P.attach_guard(S["mm"])

    CONST = plan_prelude(nc, P, S, W, SC, x_in, st)
    if stages > 0:
        plan_stage_a(nc, P, S, W, SC, CONST, pool, st)
    if stages > 1:
        plan_wkv(nc, P, S, W, SC, CONST, pool, st)
    if stages > 2:
        plan_stage_c(nc, P, S, W, SC, CONST, pool, st)
    if stages > 3:
        plan_ffn_attn(nc, P, S, W, SC, CONST, pool, st, 0)
    if stages > 4:
        plan_ffn_mlp(nc, P, S, W, SC, CONST, pool, st, 0)
    if stages > 5:
        plan_ffn_attn(nc, P, S, W, SC, CONST, pool, st, 1)
    if stages > 6:
        plan_ffn_mlp(nc, P, S, W, SC, CONST, pool, st, 1)
    if stages > 7:
        plan_final(nc, P, S, W, SC, CONST, pool, st, out)
    else:
        plan_dummy_out(nc, P, S, SC, CONST, st, out)

    barrier(P, S)

    with nc.Block() as blk:
        with nc.allow_low_precision(reason="bf16 PE transposes / operand-grade copies"):
            P.finalize(blk)
    st.close()
    return nc


def barrier(P, S):
    target = S["bar"].n + 5
    for e in Prog.ENGINES:
        P.__dict__[e].op("sem_inc", S["bar"].h, 1)
    S["bar"].inc(5)
    for e in Prog.ENGINES:
        P.__dict__[e].wait(S["bar"], target)


def plan_dummy_out(nc, P, S, SC, CONST, st, out):
    z = CONST["zf32"]
    for b in range(B):
        for j in range(T // 128):
            P.gpsimd.dma(out[b, 128 * j:128 * (j + 1), :], z[:, 0:C], S["stv"])
    P.gpsimd.wait(S["stv"], S["stv"].n)


def plan_prelude(nc, P, S, W, SC, x_in, st):
    CONST = {}
    zeros = st.enter_context(nc.sbuf_tensor("c_zeros", [128, 1024], BF16))
    zf32 = st.enter_context(nc.sbuf_tensor("c_zf32", [128, 1024], F32))
    CONST["zeros"] = zeros
    CONST["zf32"] = zf32

    cm05 = st.enter_context(nc.sbuf_tensor("c_m05", [128, 1], F32))
    ceps1 = st.enter_context(nc.sbuf_tensor("c_eps1", [128, 1], F32))
    ceps2 = st.enter_context(nc.sbuf_tensor("c_eps2", [128, 1], F32))
    cflush = st.enter_context(nc.sbuf_tensor("c_flush", [128, 1], F32))
    P.vector.flush = cflush[:]
    nc.const_aps.register(F32, -0.5, cm05[:])
    nc.const_aps.register(F32, 0.00064, ceps1[:])
    nc.const_aps.register(F32, 1e-5, ceps2[:])

    id128c = st.enter_context(nc.sbuf_tensor("c_id128", [128, 128], BF16))
    idi = st.enter_context(nc.sbuf_tensor("c_idi", [128, 128], I32))
    CONST["id128c"] = id128c
    P.gpsimd.op("iota", idi[:], [[-1, 128]], channel_multiplier=1, sem=S["io"])

    v = P.vector
    v.op("memset", zeros[:], 0.0)
    v.op("memset", cflush[:], 0.0)
    v.op("memset", cm05[:], -0.5)
    v.op("memset", ceps1[:], 0.00064)
    v.op("memset", ceps2[:], 1e-5)
    v.wait(S["io"], S["io"].n)
    v.op("tensor_single_scalar", id128c[:], idi[:], 0, ALU.is_equal)
    v.op("memset", zf32[:], 0.0, sem=S["dv"])
    dvz = S["dv"].n

    g = P.gpsimd
    g.wait(S["dv"], dvz)
    for b in range(B):
        g.dma(SC["xbf"][b, 0:TG, :], zeros[:TG, :], S["swp"])
        g.dma(SC["xbf"][b, TG:, :], x_in[b], S["swp"])
        g.dma(SC["x1"][b, 0:TG, :], zf32[:TG, :], S["swp"])
        g.dma(SC["x2"][b, 0:TG, :], zf32[:TG, :], S["swp"])
        g.dma(SC["x3"][b, 0:TG, :], zf32[:TG, :], S["swp"])
        g.dma(SC["x1b"][b, 0:TG, :], zeros[:TG, :], S["swp"])
        g.dma(SC["x2b"][b, 0:TG, :], zeros[:TG, :], S["swp"])
        g.dma(SC["x3b"][b, 0:TG, :], zeros[:TG, :], S["swp"])
    for key, wnm in (("wr_bf", "w_receptance"), ("wk_bf", "w_key"),
                     ("wv_bf", "w_value"), ("wo_bf", "w_output")):
        g.dma(SC[key], W["attn_" + wnm], S["swp"])
    CONST["sw_pre"] = S["swp"].n
    return CONST


def plan_stage_a(nc, P, S, W, SC, CONST, pool, st):
    import contextlib
    stage = contextlib.ExitStack()

    def sb(nm, sh, dt=BF16):
        return stage.enter_context(nc.sbuf_tensor(nm, sh, dt))

    g, sy, sc, v, t = P.gpsimd, P.sync, P.scalar, P.vector, P.tensor

    vec_names = ["time_maa_x", "time_maa_rg", "time_maa_wa", "time_maa_k", "time_maa_v",
                 "time_decay", "time_aaaaa", "time_misc_a", "time_misc_k"]
    vecs = {n: sb("cv_" + n, [128, NCK], F32) for n in vec_names}
    maa_w1 = sb("maa_w1", [128, NCK * 128])
    maa_w2bd = sb("maa_w2bd", [128, 4096])
    wrT = sb("wrT", [128, NCK * C])
    wkT = sb("wkT", [128, NCK * C])
    wvT = sb("wvT", [128, NCK * C])
    lora_wa = sb("lora_wa", [128, NCK * 96])
    lora_k = sb("lora_k", [128, NCK * 32])
    dw2 = sb("dw2", [64, C])
    w2s = sb("w2s", [16, 4 * C])
    gate_w1 = sb("gate_w1", [128, NCK * 128])
    gate_w2 = sb("gate_w2", [128, C])
    ind2 = sb("ind2", [128, 2])
    ind2n = sb("ind2n", [2, 128])

    v.op("memset", maa_w2bd[:], 0.0, sem=S["dv"])
    dv_w2z = S["dv"].n

    for n in vec_names:
        src = W["attn_" + n].rearrange("o i c -> (o i c)")
        for j in range(NCK):
            g.dma(vecs[n][:, j:j + 1], src[128 * j:128 * (j + 1)].unsqueeze(1), S["sw"])
    g.wait(S["dv"], dv_w2z)
    for f in range(4):
        g.dma(maa_w2bd[32 * f:32 * f + 32, 1024 * f:1024 * f + 1024],
              W["attn_time_maa_w2"][f], S["sw"])
    g.dma(maa_w1[:].rearrange("p (j f) -> p j f", j=NCK),
          W["attn_time_maa_w1"].rearrange("(j p) f -> p j f", p=128), S["sw"])
    g.dma(lora_wa[:].rearrange("p (j f) -> p j f", j=NCK)[:, :, 0:64],
          W["attn_time_decay_w1"].rearrange("(j p) f -> p j f", p=128), S["sw"])
    g.dma(lora_wa[:].rearrange("p (j f) -> p j f", j=NCK)[:, :, 64:80],
          W["attn_time_aaa_w1"].rearrange("(j p) f -> p j f", p=128), S["sw"])
    g.dma(lora_wa[:].rearrange("p (j f) -> p j f", j=NCK)[:, :, 80:96],
          W["attn_ma_w1"].rearrange("(j p) f -> p j f", p=128), S["sw"])
    g.dma(lora_k[:].rearrange("p (j f) -> p j f", j=NCK)[:, :, 0:16],
          W["attn_time_kkk_w1"].rearrange("(j p) f -> p j f", p=128), S["sw"])
    g.dma(lora_k[:].rearrange("p (j f) -> p j f", j=NCK)[:, :, 16:32],
          W["attn_mk_w1"].rearrange("(j p) f -> p j f", p=128), S["sw"])
    g.dma(dw2[:], W["attn_time_decay_w2"], S["sw"])
    for i, wnm in enumerate(["time_aaa_w2", "ma_w2", "mk_w2", "time_kkk_w2"]):
        g.dma(w2s[:, C * i:C * (i + 1)], W["attn_" + wnm], S["sw"])
    g.dma(gate_w1[:].rearrange("p (j f) -> p j f", j=NCK),
          W["attn_gate_w1"].rearrange("(j p) f -> p j f", p=128), S["sw"])
    g.dma(gate_w2[:], W["attn_gate_w2"], S["sw"])
    sw_w = S["sw"].n

    sy.wait(S["swp"], CONST["sw_pre"])
    for wT, key in ((wrT, "wr_bf"), (wkT, "wk_bf"), (wvT, "wv_bf")):
        for j in range(NCK):
            sy.dma(wT[:, C * j:C * (j + 1)], SC[key][:, 128 * j:128 * (j + 1)],
                   S["ld2"], transpose=True)
    ld_wT = S["ld2"].n

    iotp = sb("iotp", [128, 1], I32)
    iotf = sb("iotf", [2, 128], I32)
    g.op("iota", iotp[:], [[0, 1]], channel_multiplier=1)
    g.op("iota", iotf[:], [[1, 128]], channel_multiplier=-64, sem=S["io"])
    sw_io = S["io"].n
    v.wait(S["io"], sw_io)
    v.op("tensor_single_scalar", ind2[:, 0:1], iotp[:], 64, ALU.is_lt)
    v.op("tensor_single_scalar", ind2[:, 1:2], iotp[:], 64, ALU.is_ge)
    tmpi = sb("tmpi", [2, 128], F32)
    v.op("tensor_single_scalar", ind2n[:], iotf[:], 0, ALU.is_ge)
    v.op("tensor_single_scalar", tmpi[:], iotf[:], 64, ALU.is_lt)
    v.op("tensor_tensor", ind2n[:], ind2n[:], tmpi[:], ALU.mult)
    v.op("tensor_scalar_mul", ind2n[:], ind2n[:], -1.0, sem=S["dv"])
    dv_ind = S["dv"].n

    xin = [sb(f"xi{i}", [128, NCK * 640]) for i in range(2)]
    xxb = sb("xxb", [128, NCK * TT])
    t1b = sb("t1b", [128, NCK * TT])
    mixbuf = sb("mixbuf", [128, TT])
    xm = {q: sb("xm_" + q, [128, NCK * TT]) for q in ("rg", "wa", "k", "v")}
    kbuf = t1b
    kkbuf = xxb
    kk2 = sb("kk2", [128, TT])
    asig = sb("asig", [128, NCK * TT])
    k1buf = sb("k1buf", [128, 2 * TT])
    tanh_d = sb("tanh_d", [64, TT])
    tanh_kkk = sb("tanh_kkk", [16, TT])
    tanh_g = sb("tanh_g", [128, TT])
    pre2 = sb("pre2", [16, 2 * TT])
    pre_mk = sb("pre_mk", [16, TT])
    wfull = sb("wfull", [128, 2 * TT], F32)
    normp = sb("normp", [2, NCK * TT])
    masig = sb("masig", [128, TT])
    mk2s = sb("mk2s", [128, TT])
    ebuf = sb("ebuf", [128, TT])
    fp1 = sb("fp1", [128, TT], F32)
    fp2 = sb("fp2", [128, TT], F32)
    fp3 = sb("fp3", [128, TT], F32)
    stb = {n: sb("st_" + n, [128, 2 * TT]) for n in ("r", "v", "a", "b", "g")}
    rsem = {}
    for i, n in enumerate(("r", "v", "a", "b", "g", "k", "w")):
        rsem[(n, 0)] = S[f"rs{2 * i}"]
        rsem[(n, 1)] = S[f"rs{2 * i + 1}"]

    t.wait(S["ld2"], ld_wT)
    t.wait(S["sw"], sw_w)
    v.wait(S["dv"], dv_ind)

    zf32 = CONST["zf32"]

    st_iter = []
    ld_iter = []
    for it in range(B * NT):
        b, j = divmod(it, NT)
        t0 = j * TT
        pb = it % 2
        XI = xin[pb]
        XCc = lambda ck: XI[:, 640 * ck + 128:640 * ck + 640]
        XPc = lambda ck: XI[:, 640 * ck + 127:640 * ck + 639]

        for ck in range(NCK):
            sy.dma(XI[:, 640 * ck:640 * (ck + 1)],
                   SC["xbf"][b, TG + t0 - 128:TG + t0 + TT, 128 * ck:128 * (ck + 1)],
                   S["ld"], transpose=True)
        ld_iter.append(S["ld"].n)

        v.wait(S["ld"], ld_iter[-1])
        for ck in range(NCK):
            cs = slice(TT * ck, TT * (ck + 1))
            v.op("tensor_sub", xxb[:, cs], XPc(ck), XCc(ck))
            v.op("scalar_tensor_tensor", t1b[:, cs], xxb[:, cs],
                 vecs["time_maa_x"][:, ck:ck + 1], XCc(ck), ALU.mult, ALU.add,
                 sem=(S["dv"] if ck == NCK - 1 else None))
        dv_mixprep = S["dv"].n

        bi, ps = pool.acquire(t)
        t.wait(S["dv"], dv_mixprep)
        for ck in range(NCK):
            t.op("matmul", ps[0:128, 0:TT], maa_w1[:, 128 * ck:128 * (ck + 1)],
                 t1b[:, TT * ck:TT * (ck + 1)], start=(ck == 0), stop=(ck == NCK - 1),
                 sem=(S["mm"] if ck == NCK - 1 else None))
        mm_mix1 = S["mm"].n
        sc.wait(S["mm"], mm_mix1)
        ac_mix = sc.op("activation", mixbuf[:], ps[0:128, 0:TT], AF.Tanh, sem=S["ac"])
        pool.release(bi, (S["ac"], ac_mix))

        t.wait(S["ac"], ac_mix)
        for qi, q in enumerate(("rg", "wa", "k", "v")):
            vname = {"rg": "time_maa_rg", "wa": "time_maa_wa",
                     "k": "time_maa_k", "v": "time_maa_v"}[q]
            for oc in range(NCK):
                bi, ps = pool.acquire(t)
                mmv = t.op("matmul", ps[0:128, 0:TT],
                     maa_w2bd[:, 1024 * qi + 128 * oc:1024 * qi + 128 * (oc + 1)],
                     mixbuf[:], start=True, stop=True, sem=S["mm"])
                sc.wait(S["mm"], mmv)
                if S["dv"].n > dv_mixprep:
                    sc.wait(S["dv"], S["dv"].n)
                acv = sc.op("activation", fp1[:], ps[0:128, 0:TT], AF.Identity,
                            bias=vecs[vname][:, oc:oc + 1], sem=S["ac"])
                pool.release(bi, (S["ac"], acv))
                cs = slice(TT * oc, TT * (oc + 1))
                v.wait(S["ac"], acv)
                v.op("tensor_tensor", t1b[:, cs], xxb[:, cs], fp1[:], ALU.mult)
                v.op("tensor_add", xm[q][:, cs], t1b[:, cs], XCc(oc), sem=S["dv"])
        dv_xm = S["dv"].n
        t.wait(S["dv"], dv_xm)

        def store_oc(name, dst_sc, buf, psl, oc):
            g.dma(dst_sc[b, 128 * oc:128 * (oc + 1), t0:t0 + TT], buf[:, psl],
                  rsem[(name, oc % 2)])

        def rot_wait(eng, name, oc):
            eng.wait(rsem[(name, oc % 2)], rsem[(name, oc % 2)].n)

        def proj(wT, xsrc, name):
            for oc in range(NCK):
                bi, ps = pool.acquire(t)
                mmv = None
                for ck in range(NCK):
                    mmv = t.op("matmul", ps[0:128, 0:TT],
                               wT[:, C * ck + 128 * oc:C * ck + 128 * (oc + 1)],
                               xsrc[:, TT * ck:TT * (ck + 1)],
                               start=(ck == 0), stop=(ck == NCK - 1),
                               sem=(S["mm"] if ck == NCK - 1 else None))
                psl = slice(TT * (oc % 2), TT * (oc % 2 + 1))
                v.wait(S["mm"], mmv)
                rot_wait(v, name, oc)
                dvv = v.op("tensor_copy", stb[name][:, psl], ps[0:128, 0:TT],
                           sem=S["dv"])
                pool.release(bi, (S["dv"], dvv))
                g.wait(S["dv"], dvv)
                store_oc(name, SC[name], stb[name], psl, oc)

        proj(wrT, xm["rg"], "r")
        proj(wvT, xm["v"], "v")

        bi_kk, ps_kk = pool.acquire(t)
        bi_mk, ps_mk = pool.acquire(t)
        for ck in range(NCK):
            t.op("matmul", ps_kk[0:16, 0:TT],
                 lora_k[:, 32 * ck:32 * ck + 16], xm["k"][:, TT * ck:TT * (ck + 1)],
                 start=(ck == 0), stop=(ck == NCK - 1))
        mm_lk = None
        for ck in range(NCK):
            mm_lk = t.op("matmul", ps_mk[0:16, 0:TT],
                         lora_k[:, 32 * ck + 16:32 * ck + 32],
                         xm["k"][:, TT * ck:TT * (ck + 1)],
                         start=(ck == 0), stop=(ck == NCK - 1),
                         sem=(S["mm"] if ck == NCK - 1 else None))
        sc.wait(S["mm"], mm_lk)
        ac_tk = sc.op("activation", tanh_kkk[:], ps_kk[0:16, 0:TT], AF.Tanh, sem=S["ac"])
        pool.release(bi_kk, (S["ac"], ac_tk))
        v.wait(S["mm"], mm_lk)
        dv_mk = v.op("tensor_copy", pre_mk[:], ps_mk[0:16, 0:TT], sem=S["dv"])
        pool.release(bi_mk, (S["dv"], dv_mk))

        bi_d, ps_d = pool.acquire(t)
        bi_a, ps_a = pool.acquire(t)
        bi_m, ps_m = pool.acquire(t)
        for ck in range(NCK):
            t.op("matmul", ps_d[0:64, 0:TT], lora_wa[:, 96 * ck:96 * ck + 64],
                 xm["wa"][:, TT * ck:TT * (ck + 1)], start=(ck == 0), stop=(ck == NCK - 1))
        for ck in range(NCK):
            t.op("matmul", ps_a[0:16, 0:TT], lora_wa[:, 96 * ck + 64:96 * ck + 80],
                 xm["wa"][:, TT * ck:TT * (ck + 1)], start=(ck == 0), stop=(ck == NCK - 1))
        mm_lwa = None
        for ck in range(NCK):
            mm_lwa = t.op("matmul", ps_m[0:16, 0:TT], lora_wa[:, 96 * ck + 80:96 * ck + 96],
                          xm["wa"][:, TT * ck:TT * (ck + 1)],
                          start=(ck == 0), stop=(ck == NCK - 1),
                          sem=(S["mm"] if ck == NCK - 1 else None))
        sc.wait(S["mm"], mm_lwa)
        ac_td = sc.op("activation", tanh_d[:], ps_d[0:64, 0:TT], AF.Tanh, sem=S["ac"])
        pool.release(bi_d, (S["ac"], ac_td))
        v.wait(S["mm"], mm_lwa)
        v.op("tensor_copy", pre2[:, 0:TT], ps_a[0:16, 0:TT])
        dv_pre2 = v.op("tensor_copy", pre2[:, TT:2 * TT], ps_m[0:16, 0:TT], sem=S["dv"])
        pool.release(bi_a, (S["dv"], dv_pre2))
        pool.release(bi_m, (S["dv"], dv_pre2))

        bi_g, ps_g = pool.acquire(t)
        mm_g1 = None
        for ck in range(NCK):
            mm_g1 = t.op("matmul", ps_g[0:128, 0:TT], gate_w1[:, 128 * ck:128 * (ck + 1)],
                         xm["rg"][:, TT * ck:TT * (ck + 1)],
                         start=(ck == 0), stop=(ck == NCK - 1),
                         sem=(S["mm"] if ck == NCK - 1 else None))
        sc.wait(S["mm"], mm_g1)
        ac_tg = sc.op("activation", tanh_g[:], ps_g[0:128, 0:TT], AF.Tanh, sem=S["ac"])
        pool.release(bi_g, (S["ac"], ac_tg))

        t.wait(S["ac"], ac_tk)
        bi_nm, ps_nm = pool.acquire(t)
        dvn_prev = None
        ac_kk_prev = None
        for oc in range(NCK):
            cs = slice(TT * oc, TT * (oc + 1))
            bi, ps = pool.acquire(t)
            for ck in range(NCK):
                t.op("matmul", ps[0:128, 0:TT],
                     wkT[:, C * ck + 128 * oc:C * ck + 128 * (oc + 1)],
                     xm["k"][:, TT * ck:TT * (ck + 1)],
                     start=(ck == 0), stop=False)
            mmv = t.op("matmul", ps[0:128, 0:TT],
                       w2s[:, 3 * C + 128 * oc:3 * C + 128 * (oc + 1)],
                       tanh_kkk[:], start=False, stop=True, sem=S["mm"])
            bi_l, ps_l = pool.acquire(t)
            mml = t.op("matmul", ps_l[0:128, 0:TT],
                       w2s[:, 3 * C + 128 * oc:3 * C + 128 * (oc + 1)],
                       tanh_kkk[:], start=True, stop=True, sem=S["mm"])
            v.wait(S["mm"], mml)
            dvv2 = v.op("tensor_copy", kkbuf[:, cs], ps[0:128, 0:TT], sem=S["dv"])
            dvv = v.op("tensor_sub", kbuf[:, cs], kkbuf[:, cs], ps_l[0:128, 0:TT],
                       sem=S["dv"])
            pool.release(bi_l, (S["dv"], dvv))
            sc.wait(S["dv"], dvv2)
            if ac_kk_prev is not None:
                sc.wait(S["mm"], mm_nm_prev)
            ac_sq = sc.op("activation", kk2[:], ps[0:128, 0:TT], AF.Square, sem=S["ac"])
            pool.release(bi, (S["ac"], ac_sq))
            if dvn_prev is not None:
                t.wait(S["dv"], dvn_prev)
            t.wait(S["ac"], ac_sq)
            mm_nm_prev = t.op("matmul", ps_nm[0:2, 0:TT], ind2[:], kk2[:],
                              start=True, stop=True, sem=S["mm"])
            v.wait(S["mm"], mm_nm_prev)
            dvn_prev = v.op("tensor_copy", normp[:, cs], ps_nm[0:2, 0:TT], sem=S["dv"])
            ac_kk_prev = ac_sq
        pool.release(bi_nm, (S["dv"], dvn_prev))

        v.wait(S["dv"], dvn_prev)
        dv_nm = v.op("tensor_scalar_max", normp[:], normp[:], 1e-24, sem=S["dv"])
        sc.wait(S["dv"], dv_nm)
        sc.op("activation", normp[:], normp[:], AF.Ln)
        ac_rn = sc.op("activation", normp[:], normp[:], AF.Exp, scale=-0.5, sem=S["ac"])

        t.wait(S["ac"], ac_rn)
        t.wait(S["dv"], dv_pre2)
        dv_oc_prev = None
        for oc in range(NCK):
            cs = slice(TT * oc, TT * (oc + 1))
            psl = slice(TT * (oc % 2), TT * (oc % 2 + 1))
            bi1, ps1 = pool.acquire(t)
            t.op("matmul", ps1[0:128, 0:TT], dw2[:, 128 * oc:128 * (oc + 1)],
                 tanh_d[:], start=True, stop=True)
            bi2, ps2 = pool.acquire(t)
            t.op("matmul", ps2[0:128, 0:TT],
                 w2s[:, 0 * C + 128 * oc:0 * C + 128 * (oc + 1)],
                 pre2[:, 0:TT], start=True, stop=True)
            bi3, ps3 = pool.acquire(t)
            t.op("matmul", ps3[0:128, 0:TT],
                 w2s[:, 1 * C + 128 * oc:1 * C + 128 * (oc + 1)],
                 pre2[:, TT:2 * TT], start=True, stop=True)
            bi4, ps4 = pool.acquire(t)
            mmv = t.op("matmul", ps4[0:128, 0:TT],
                       w2s[:, 2 * C + 128 * oc:2 * C + 128 * (oc + 1)],
                       pre_mk[:], start=True, stop=True, sem=S["mm"])
            bi5, ps5 = pool.acquire(t)
            mmb = t.op("matmul", ps5[0:128, 0:TT], ind2n[:], normp[:, cs],
                       start=True, stop=True, sem=S["mm"])

            sc.wait(S["mm"], mmv)
            if dv_oc_prev is not None:
                sc.wait(S["dv"], dv_oc_prev)
            rot_wait(sc, "w", oc)
            sc.op("activation", fp1[:], ps1[0:128, 0:TT], AF.Identity,
                  bias=vecs["time_decay"][:, oc:oc + 1])
            sc.op("activation", asig[:, cs], ps2[0:128, 0:TT], AF.Sigmoid,
                  bias=vecs["time_aaaaa"][:, oc:oc + 1])
            sc.op("activation", masig[:], ps3[0:128, 0:TT], AF.Sigmoid,
                  bias=vecs["time_misc_a"][:, oc:oc + 1])
            acv = sc.op("activation", mk2s[:], ps4[0:128, 0:TT], AF.Sigmoid,
                        bias=vecs["time_misc_k"][:, oc:oc + 1], sem=S["ac"])
            pool.release(bi1, (S["ac"], acv))
            pool.release(bi2, (S["ac"], acv))
            pool.release(bi3, (S["ac"], acv))
            pool.release(bi4, (S["ac"], acv))

            v.wait(S["ac"], acv)
            v.op("tensor_scalar_mul", fp2[:], fp1[:], -1.0)
            dvw = v.op("tensor_max", fp2[:], fp2[:], fp1[:], sem=S["dv"])
            sc.wait(S["dv"], dvw)
            sc.op("activation", fp3[:], fp2[:], AF.Exp, scale=-1.0)
            ac_l = sc.op("activation", fp3[:], fp3[:], AF.Ln, bias=1.0, sem=S["ac"])
            v.op("scalar_tensor_tensor", fp2[:], fp1[:], -1.0, zf32[:, 0:TT],
                 ALU.mult, ALU.max)
            v.wait(S["ac"], ac_l)
            dvw2 = v.op("tensor_add", fp2[:], fp2[:], fp3[:], sem=S["dv"])
            sc.wait(S["dv"], dvw2)
            ac_w = sc.op("activation", wfull[:, psl], fp2[:], AF.Identity,
                         scale=-1.0, bias=-0.5, sem=S["ac"])
            g.wait(S["ac"], ac_w)
            store_oc("w", SC["w"], wfull, psl, oc)

            v.op("tensor_tensor", fp1[:], asig[:, cs], masig[:], ALU.mult)
            v.op("tensor_add", fp2[:], asig[:, cs], masig[:])
            v.op("tensor_sub", fp2[:], fp2[:], fp1[:])
            rot_wait(v, "k", oc)
            v.op("tensor_tensor", k1buf[:, psl], kbuf[:, cs], fp2[:], ALU.mult)
            v.wait(S["ac"], ac_w)
            v.op("tensor_tensor", fp1[:], wfull[:, psl], mk2s[:], ALU.mult)
            dv_e = v.op("tensor_single_scalar", fp1[:], fp1[:], 0.0, ALU.min,
                        sem=S["dv"])
            sc.wait(S["dv"], dv_e)
            ac_e = sc.op("activation", ebuf[:], fp1[:], AF.Exp, sem=S["ac"])
            v.wait(S["ac"], ac_e)
            dvk = v.op("tensor_tensor", k1buf[:, psl], k1buf[:, psl], ebuf[:],
                       ALU.mult, sem=S["dv"])
            g.wait(S["dv"], dvk)
            store_oc("k", SC["k"], k1buf, psl, oc)

            v.wait(S["mm"], mmb)
            rot_wait(v, "a", oc)
            rot_wait(v, "b", oc)
            v.op("tensor_tensor", stb["a"][:, psl], kkbuf[:, cs], ps5[0:128, 0:TT],
                 ALU.mult)
            dvab = v.op("scalar_tensor_tensor", stb["b"][:, psl], stb["a"][:, psl],
                        -1.0, asig[:, cs], ALU.mult, ALU.mult, sem=S["dv"])
            pool.release(bi5, (S["dv"], dvab))
            g.wait(S["dv"], dvab)
            store_oc("a", SC["a"], stb["a"], psl, oc)
            store_oc("b", SC["b"], stb["b"], psl, oc)
            dv_oc_prev = dvab

        t.wait(S["ac"], ac_tg)
        for oc in range(NCK):
            psl = slice(TT * (oc % 2), TT * (oc % 2 + 1))
            bi, ps = pool.acquire(t)
            mmv = t.op("matmul", ps[0:128, 0:TT], gate_w2[:, 128 * oc:128 * (oc + 1)],
                       tanh_g[:], start=True, stop=True, sem=S["mm"])
            v.wait(S["mm"], mmv)
            rot_wait(v, "g", oc)
            dvv = v.op("tensor_copy", stb["g"][:, psl], ps[0:128, 0:TT], sem=S["dv"])
            pool.release(bi, (S["dv"], dvv))
            g.wait(S["dv"], dvv)
            store_oc("g", SC["g"], stb["g"], psl, oc)

        st_iter.append(0)
        for sm in set(rsem.values()):
            g.wait(sm, sm.n)
        barrier(P, S)

    CONST["a_done"] = {"stv": S["stv"].n}
    stage.close()


def plan_wkv(nc, P, S, W, SC, CONST, pool, st):
    import contextlib
    stage = contextlib.ExitStack()

    def sb(nm, sh, dt=BF16):
        return stage.enter_context(nc.sbuf_tensor(nm, sh, dt))

    g, sy, sc, v, t = P.gpsimd, P.sync, P.scalar, P.vector, P.tensor
    zf32 = CONST["zf32"]

    CGT = 512
    NCG = T // CGT
    CPG = CGT // L

    SD = {}
    for n in ("v", "a", "b", "k", "r", "g"):
        SD[n] = nc.dram_tensor("sT_" + n, [B, T, C], BF16,
                               kind="ExternalOutput" if DEBUG_TAPS else "Internal").ap()
    tps = [sb("tp0", [128, 1024]), sb("tp1", [128, 1024])]
    st_sem = [S["stv"], S["swp"]]
    st_tp = {}
    for it, (b, key, tb) in enumerate([(b, key, tb) for b in range(B)
                                       for key in ("v", "a", "b", "k", "r", "g")
                                       for tb in range(T // 128)]):
        pb = it % 2
        buf = tps[pb]
        if it >= 2:
            sy.wait(st_sem[pb], st_tp[pb])
        sy.dma(buf[:], SC[key][b, 0:C, 128 * tb:128 * (tb + 1)],
               S["ld2"], transpose=True)
        g.wait(S["ld2"], S["ld2"].n)
        st_tp[pb] = g.dma(SD[key][b, 128 * tb:128 * (tb + 1), :], buf[:],
                          st_sem[pb])
    g.wait(S["stv"], S["stv"].n)
    g.wait(S["swp"], S["swp"].n)
    barrier(P, S)

    iotm4 = sb("w_iotm4", [128, 256], I32)
    maskS4 = sb("w_maskS4", [128, 256], U8)
    maskI4 = sb("w_maskI4", [128, 256], U8)
    iotbd = sb("w_iotbd", [128, 128], I32)
    mbd1 = sb("w_mbd1", [128, 128], U8)
    maskBD = sb("w_maskBD", [128, 128], U8)
    id128 = sb("w_id128", [128, 128])
    iot128 = sb("w_iot128", [128, 128], I32)
    g.op("iota", iotm4[:].rearrange("p (i x) -> p i x", i=4), [[0, 4], [-1, 64]],
         channel_multiplier=1)
    g.op("iota", iotbd[:].rearrange("p (fb x) -> p fb x", fb=2), [[-64, 2], [0, 64]],
         channel_multiplier=1)
    g.op("iota", iot128[:], [[-1, 128]], channel_multiplier=1, sem=S["io"])
    io_w = S["io"].n
    v.wait(S["io"], io_w)
    v.op("tensor_single_scalar", iotm4[64:128, :], iotm4[64:128, :], 64, ALU.subtract)
    v.op("tensor_single_scalar", maskS4[:], iotm4[:], 0, ALU.is_lt)
    v.op("tensor_single_scalar", maskI4[:], iotm4[:], 0, ALU.is_le)
    v.op("tensor_single_scalar", id128[:], iot128[:], 0, ALU.is_equal)
    v.op("tensor_single_scalar", mbd1[:], iotbd[:], 0, ALU.is_ge)
    v.op("tensor_single_scalar", maskBD[:], iotbd[:], 64, ALU.is_lt)
    v.op("tensor_tensor", maskBD[:], maskBD[:], mbd1[:], ALU.mult, sem=S["dv"])
    dv_c = S["dv"].n

    GB = CPG * 65
    gbuf = sb("w_gbuf", [128, GB], F32)
    esub = sb("w_esub", [128, GB], F32)
    efp = sb("w_efp", [128, GB], F32)
    ebf = sb("w_ebf", [128, GB])
    einv = sb("w_einv", [128, GB])
    esubX = sb("w_esubX", [128, CPG * 64], F32)
    ebfX = sb("w_ebfX", [128, CPG * 64])
    rho2 = sb("w_rho2", [128, CPG * 4], F32)
    erho = sb("w_erho", [128, CPG * 4])
    sclsub = sb("w_scls", [128, CPG * 4], F32)
    sclb = sb("w_sclb", [128, CPG * 4])
    gl = sb("w_gl", [128, CPG], F32)
    elb = sb("w_elb", [128, CPG])
    svec = sb("w_svec", [128, CPG * 8], F32)
    esv = sb("w_esv", [128, CPG * 8])
    rn = sb("w_rn", [128, CGT]); an = sb("w_an", [128, CGT])
    bn = sb("w_bn", [128, CGT]); kn = sb("w_kn", [128, CGT])
    wn = sb("w_wn", [128, CGT], F32)
    vt2 = sb("w_vt2", [128, CPG * 128]); at2 = sb("w_at2", [128, CPG * 128])
    bt2 = sb("w_bt2", [128, CPG * 128]); kt2 = sb("w_kt2", [128, CPG * 128])
    amid = sb("w_amid", [128, CGT]); rmid = sb("w_rmid", [128, CGT])
    bmidBD = sb("w_bmidBD", [128, CPG * 128]); kmidBD = sb("w_kmidBD", [128, CPG * 128])
    b31BD = sb("w_b31BD", [128, CPG * 64]); k31BD = sb("w_k31BD", [128, CPG * 64])
    b15BD = sb("w_b15BD", [128, CPG * 32]); k15BD = sb("w_k15BD", [128, CPG * 32])
    b47BD = sb("w_b47BD", [128, CPG * 32]); k47BD = sb("w_k47BD", [128, CPG * 32])
    R1 = sb("w_R1", [128, CPG * 256])
    FAn = sb("w_FAn", [128, CGT]); FBn = sb("w_FBn", [128, CGT])
    FAt2 = sb("w_FAt2", [128, CPG * 128]); FBt2 = sb("w_FBt2", [128, CPG * 128])
    abar2 = sb("w_abar2", [128, CPG * 128])
    bhat2 = sb("w_bhat2", [128, CPG * 128])
    bhatBD = sb("w_bhatBD", [128, CPG * 128]); khatZ = sb("w_khatZ", [128, CPG * 128])
    rbar = sb("w_rbar", [128, CGT])
    A0m = sb("w_A0m", [128, CPG * 128]); B0m = sb("w_B0m", [128, CPG * 128])
    akm = sb("w_akm", [128, CPG * 128])
    prbm = sb("w_prbm", [128, CPG * 128]); prkm = sb("w_prkm", [128, CPG * 128])
    pA1 = sb("w_pA1", [128, CPG * 128]); pA2 = sb("w_pA2", [128, CPG * 128])
    pB1 = sb("w_pB1", [128, CPG * 128]); pB2 = sb("w_pB2", [128, CPG * 128])
    pB3 = sb("w_pB3", [128, CPG * 128])
    Gb = [sb(f"w_G{i}", [128, CPG * 128]) for i in range(2)]
    mkv = sb("w_mkv", [128, CPG * 128])
    Xb = sb("w_Xb", [128, CPG * 256])
    Db = sb("w_Db", [128, CPG * 128])
    Ub = sb("w_Ub", [128, CPG * 64])
    RfBD = sb("w_RfBD", [128, CPG * 128])
    Abd = sb("w_Abd", [128, CPG * 128])
    state = sb("w_state", [128, 64])
    ycg = sb("w_ycg", [128, CPG * 64])
    xtr = sb("w_xtr", [128, 256])
    xsel = sb("w_xsel", [128, 256])
    ylocb = sb("w_ylocb", [128, CPG * 64])

    t.wait(S["dv"], dv_c)

    r5 = lambda bb, inner=64: bb[:].rearrange("p (c x) -> p c x", x=inner)
    r65 = lambda bb: bb[:].rearrange("p (c x) -> p c x", x=65)
    rbl = lambda bb: bb[:].rearrange("p (c bl q) -> p c bl q", c=CPG, bl=4)
    rb4 = lambda bb: bb[:].rearrange("p (c bl) -> p c bl", bl=4)
    bc16 = lambda ap3: ap3.unsqueeze(3).broadcast_to(list(ap3.shape) + [16])
    hrow = lambda bb, h: bb[64 * h:64 * h + 64, :]

    npair = WKV_PAIRS if WKV_PAIRS else B * 8
    ncgs = WKV_CGS if WKV_CGS else NCG
    for pair in range(npair):
        b, hp = divmod(pair, 8)
        c0 = 128 * hp
        v.op("memset", Abd[:], 0.0)
        v.op("memset", bmidBD[:], 0.0)
        v.op("memset", kmidBD[:], 0.0)
        v.op("memset", b31BD[:], 0.0)
        v.op("memset", k31BD[:], 0.0)
        v.op("memset", b15BD[:], 0.0)
        v.op("memset", k15BD[:], 0.0)
        v.op("memset", b47BD[:], 0.0)
        v.op("memset", k47BD[:], 0.0)
        v.op("memset", khatZ[:], 0.0)
        v.op("memset", bhatBD[:], 0.0)
        v.op("memset", A0m[:], 0.0)
        v.op("memset", akm[:], 0.0)
        v.op("memset", prbm[:], 0.0)
        v.op("memset", prkm[:], 0.0)
        v.op("memset", RfBD[:], 0.0)
        v.op("memset", Db[:], 0.0)
        v.op("memset", gbuf[:], 0.0)
        v.op("memset", state[:], 0.0, sem=S["dv"])
        dv_init = S["dv"].n
        t.wait(S["dv"], dv_init)

        for cg in range(ncgs):
            t0 = cg * CGT
            tsl = slice(t0, t0 + CGT)
            for buf, key in ((rn, "r"), (an, "a"), (bn, "b"), (kn, "k")):
                sy.dma(buf[:], SC[key][b, c0:c0 + 128, tsl], S["ld"])
            sy.dma(wn[:], SC["w"][b, c0:c0 + 128, tsl], S["ld"])
            for buf, key in ((vt2, "v"), (at2, "a"), (bt2, "b"), (kt2, "k")):
                for dup in range(2):
                    sy.dma(buf[64 * dup:64 * dup + 64, :]
                           .rearrange("p (c x) -> p c x", x=128),
                           SD[key][b, tsl, c0:c0 + 128]
                           .rearrange("(c p) x -> p c x", p=64), S["ld"])
            ld_cg = S["ld"].n
            v.wait(S["ld"], ld_cg)
            t.wait(S["ld"], ld_cg)

            for c in range(CPG):
                v.op("tensor_tensor_scan", gbuf[:, 65 * c + 1:65 * c + 65],
                     wn[:, 64 * c:64 * (c + 1)], zf32[:, 0:64], 0.0, ALU.add, ALU.add)
            v.op("tensor_tensor", rb4(rho2), r65(gbuf)[:, :, 0:64:16],
                 r65(gbuf)[:, :, 16:65:16], ALU.add)
            v.op("tensor_copy", gl[:].unsqueeze(2), r65(gbuf)[:, :, 64:65])
            v.op("scalar_tensor_tensor",
                 r65(esub)[:, :, 1:65].rearrange("p c (bl q) -> p c bl q", bl=4),
                 bc16(rb4(rho2)), -0.5,
                 r65(gbuf)[:, :, 1:65].rearrange("p c (bl q) -> p c bl q", bl=4),
                 ALU.mult, ALU.add)
            v.op("tensor_scalar_mul", r65(esub)[:, :, 0:1], rb4(rho2)[:, :, 0:1], -0.5)
            v.op("scalar_tensor_tensor",
                 esubX[:].rearrange("p (c bl q) -> p c bl q", c=CPG, bl=4),
                 bc16(rb4(rho2)), -0.5,
                 r65(gbuf)[:, :, 0:64].rearrange("p c (bl q) -> p c bl q", bl=4),
                 ALU.mult, ALU.add)
            v.op("scalar_tensor_tensor", rb4(sclsub), rb4(rho2), -0.5,
                 gl[:].unsqueeze(2).broadcast_to([128, CPG, 4]), ALU.mult, ALU.add)
            sv = svec[:].rearrange("p (c q) -> p c q", q=8)
            gref = lambda pos: r65(gbuf)[:, :, pos:pos + 1].broadcast_to([128, CPG, 1])
            for q, (bl, pos, sign) in enumerate(
                    [(2, 32, +1), (3, 32, +1), (0, 32, -1), (1, 32, -1),
                     (1, 16, +1), (0, 16, -1), (3, 48, +1), (2, 48, -1)]):
                v.op("scalar_tensor_tensor", sv[:, :, q:q + 1],
                     rb4(rho2)[:, :, bl:bl + 1], 0.5 * sign, gref(pos),
                     ALU.mult, ALU.subtract if sign > 0 else ALU.add)
            dv_w2a = v.op("engine_nop", sem=S["dv"])
            sc.wait(S["dv"], dv_w2a)
            sc.op("activation", efp[:], esub[:], AF.Exp)
            sc.op("activation", ebfX[:], esubX[:], AF.Exp)
            sc.op("activation", erho[:], rho2[:], AF.Exp, scale=0.5)
            sc.op("activation", sclb[:], sclsub[:], AF.Exp)
            sc.op("activation", elb[:], gl[:], AF.Exp)
            sc.op("activation", esv[:], svec[:], AF.Exp, sem=S["ac"])
            ac_w2 = S["ac"].n
            v.wait(S["ac"], ac_w2)
            v.op("tensor_copy", ebf[:], efp[:])
            v.op("reciprocal", einv[:], efp[:])
            EIN = r65(ebf)[:, :, 1:65]
            VIN = r65(einv)[:, :, 1:65]
            v.op("tensor_tensor", r5(amid), r5(an), r5(ebfX), ALU.mult)
            v.op("tensor_tensor", r5(rmid), r5(rn), EIN, ALU.mult)
            v.op("tensor_tensor", rbl(rbar), rbl(rmid), bc16(rb4(erho)), ALU.mult)
            v.op("tensor_tensor", rbl(FAn), rbl(ebfX), bc16(rb4(erho)), ALU.mult)
            v.op("tensor_tensor", rbl(FBn),
                 r65(einv)[:, :, 1:65].rearrange("p c (bl q) -> p c bl q", bl=4),
                 bc16(rb4(sclb)), ALU.mult)
            for h in range(2):
                v.op("tensor_tensor",
                     Db[64 * h:64 * h + 64, :]
                     .rearrange("p (c two x) -> p c two x", two=2, x=64)[:, :, h, :],
                     id128[64 * h:64 * h + 64, 64 * h:64 * h + 64]
                     .unsqueeze(1).broadcast_to([64, CPG, 64]),
                     elb[64 * h:64 * h + 64, :].unsqueeze(2)
                     .broadcast_to([64, CPG, 64]), ALU.mult)
            R1r = r5(R1, 256)
            ir = lambda two: R1r[:, :, 0:128].rearrange(
                "p c (bl two q) -> p c bl two q", bl=4, two=2)[:, :, :, two, :]
            v.op("tensor_copy", ir(0), rbl(amid))
            v.op("tensor_copy", ir(1), rbl(rmid))
            esvr = esv[:].rearrange("p (c q) -> p c q", q=8)
            off32 = R1r[:, :, 128:192].rearrange(
                "p c (bl two q) -> p c bl two q", bl=2, two=2)
            v.op("tensor_tensor", off32[:, :, :, 0, :], rbl(amid)[:, :, 2:4, :],
                 esvr[:, :, 0:2].unsqueeze(3).broadcast_to([128, CPG, 2, 16]), ALU.mult)
            v.op("tensor_tensor", off32[:, :, :, 1, :], rbl(rmid)[:, :, 2:4, :],
                 esvr[:, :, 0:2].unsqueeze(3).broadcast_to([128, CPG, 2, 16]), ALU.mult)
            v.op("tensor_tensor", R1r[:, :, 192:208], rbl(amid)[:, :, 1, :],
                 esvr[:, :, 4:5].broadcast_to([128, CPG, 16]), ALU.mult)
            v.op("tensor_tensor", R1r[:, :, 208:224], rbl(rmid)[:, :, 1, :],
                 esvr[:, :, 4:5].broadcast_to([128, CPG, 16]), ALU.mult)
            v.op("tensor_tensor", R1r[:, :, 224:240], rbl(amid)[:, :, 3, :],
                 esvr[:, :, 6:7].broadcast_to([128, CPG, 16]), ALU.mult)
            v.op("tensor_tensor", R1r[:, :, 240:256], rbl(rmid)[:, :, 3, :],
                 esvr[:, :, 6:7].broadcast_to([128, CPG, 16]), ALU.mult)
            for h in range(2):
                hs = slice(64 * h, 64 * h + 64)
                bdc = lambda bb, wj: bb[hs, :].rearrange(
                    "p (c two x) -> p c two x", two=2, x=wj)[:, :, h, :]
                vin_h = r65(einv)[hs, :, 1:65]
                v.op("tensor_tensor", bdc(bmidBD, 64), r5(bn, 64)[hs], vin_h, ALU.mult)
                v.op("tensor_tensor", bdc(kmidBD, 64), r5(kn, 64)[hs], vin_h, ALU.mult)
            dv_bd = plan_wkv_bdver(nc, P, S, locals())
            plan_wkv_core(nc, P, S, pool, locals())

        barrier(P, S)
    stage.close()
    CONST["SD"] = SD


def plan_wkv_bdver(nc, P, S, env):
    v = P.vector
    CPG = env["CPG"]
    r65, rbl, r5 = env["r65"], env["rbl"], env["r5"]
    einv, esv = env["einv"], env["esv"]
    bn, kn = env["bn"], env["kn"]
    b31BD, k31BD = env["b31BD"], env["k31BD"]
    b15BD, k15BD = env["b15BD"], env["k15BD"]
    b47BD, k47BD = env["b47BD"], env["k47BD"]
    S_ = S
    for h in range(2):
        hs = slice(64 * h, 64 * h + 64)
        vinb = lambda blsl: r65(einv)[hs, :, 1:65].rearrange(
            "p c (bl q) -> p c bl q", bl=4)[:, :, blsl, :]
        esvrh = esv[hs, :].rearrange("p (c q) -> p c q", q=8)
        src_b = lambda blsl: r5(bn, 64)[hs].rearrange(
            "p c (bl q) -> p c bl q", bl=4)[:, :, blsl, :]
        src_k = lambda blsl: r5(kn, 64)[hs].rearrange(
            "p c (bl q) -> p c bl q", bl=4)[:, :, blsl, :]
        d31 = lambda bb: bb[hs, :].rearrange(
            "p (c two x) -> p c two x", two=2, x=32)[:, :, h, :].rearrange(
            "p c (bl q) -> p c bl q", bl=2)
        sc31 = esvrh[:, :, 2:4].unsqueeze(3).broadcast_to([64, CPG, 2, 16])
        v.op("tensor_tensor", d31(b31BD), src_b(slice(0, 2)), vinb(slice(0, 2)),
             ALU.mult)
        v.op("tensor_tensor", d31(b31BD), d31(b31BD), sc31, ALU.mult)
        v.op("tensor_tensor", d31(k31BD), src_k(slice(0, 2)), vinb(slice(0, 2)),
             ALU.mult)
        v.op("tensor_tensor", d31(k31BD), d31(k31BD), sc31, ALU.mult)
        d15 = lambda bb: bb[hs, :].rearrange(
            "p (c two x) -> p c two x", two=2, x=16)[:, :, h, :]
        sc15 = esvrh[:, :, 5:6].broadcast_to([64, CPG, 16])
        v.op("tensor_tensor", d15(b15BD), src_b(0), vinb(0), ALU.mult)
        v.op("tensor_tensor", d15(b15BD), d15(b15BD), sc15, ALU.mult)
        v.op("tensor_tensor", d15(k15BD), src_k(0), vinb(0), ALU.mult)
        v.op("tensor_tensor", d15(k15BD), d15(k15BD), sc15, ALU.mult)
        d47 = lambda bb: bb[hs, :].rearrange(
            "p (c two x) -> p c two x", two=2, x=16)[:, :, h, :]
        sc47 = esvrh[:, :, 7:8].broadcast_to([64, CPG, 16])
        v.op("tensor_tensor", d47(b47BD), src_b(2), vinb(2), ALU.mult)
        v.op("tensor_tensor", d47(b47BD), d47(b47BD), sc47, ALU.mult)
        v.op("tensor_tensor", d47(k47BD), src_k(2), vinb(2), ALU.mult)
        v.op("tensor_tensor", d47(k47BD), d47(k47BD), sc47, ALU.mult)
    return v.op("engine_nop", sem=S["dv"])


def plan_wkv_core(nc, P, S, pool, env):
    g, sy, sc, v, t = P.gpsimd, P.sync, P.scalar, P.vector, P.tensor
    E = env
    CPG = E["CPG"]
    bdh = lambda bb, h, w=64: bb[64 * h:64 * h + 64, :].rearrange(
        "p (c two x) -> p c two x", two=2, x=w)[:, :, h, :]

    t.wait(S["dv"], E["dv_bd"])
    for half in range(2):
        bi, ps = pool.acquire(t)
        pb16 = ps[:].bitcast(BF16)
        mmv = None
        for ci in range(4):
            c = 4 * half + ci
            for src_, off in ((E["FAn"], 256 * ci), (E["FBn"], 256 * ci + 128)):
                for dup in range(2):
                    mmv = t.op("transpose",
                               pb16[64 * dup:64 * dup + 64, off:off + 128],
                               src_[:, 64 * c:64 * (c + 1)], E["id128"][:],
                               tile_position=(0, 64 * dup),
                               sem=(S["mm"] if (ci == 3 and off % 256 == 128
                                                and dup == 1) else None))
        v.wait(S["mm"], mmv)
        for ci in range(4):
            c = 4 * half + ci
            v.op("tensor_copy", E["FAt2"][:, 128 * c:128 * (c + 1)],
                 pb16[:, 256 * ci:256 * ci + 128])
            v.op("tensor_copy", E["FBt2"][:, 128 * c:128 * (c + 1)],
                 pb16[:, 256 * ci + 128:256 * ci + 256])
        dvv = v.op("engine_nop", sem=S["dv"])
        pool.release(bi, (S["dv"], dvv))
    v.op("tensor_tensor", E["abar2"][:], E["at2"][:], E["FAt2"][:], ALU.mult)
    v.op("tensor_tensor", E["bhat2"][:], E["bt2"][:], E["FBt2"][:], ALU.mult)
    for h in range(2):
        v.op("tensor_tensor", bdh(E["bhatBD"], h), bdh(E["bt2"], h),
             bdh(E["FBt2"], h), ALU.mult)
    v.op("tensor_tensor", E["khatZ"][0:64, :], E["kt2"][0:64, :],
         E["FBt2"][0:64, :], ALU.mult)
    dv_w3 = v.op("engine_nop", sem=S["dv"])

    t.wait(S["dv"], dv_w3)
    for bk in range(2):
        biA, psA = pool.acquire(t)
        biK, psK = pool.acquire(t)
        mmv = None
        for si, (psX, cmid, c31, c15, c47) in enumerate((
                (psA, E["bmidBD"], E["b31BD"], E["b15BD"], E["b47BD"]),
                (psK, E["kmidBD"], E["k31BD"], E["k15BD"], E["k47BD"]))):
            for ci in range(4):
                c = 4 * bk + ci
                off = 128 * ci
                t.op("matmul", psX[0:128, off:off + 128],
                     cmid[:, 128 * c:128 * (c + 1)],
                     E["R1"][:, 256 * c:256 * c + 128], start=True, stop=True)
                for h in range(2):
                    t.op("matmul", psX[64 * h:64 * h + 32, off + 64:off + 128],
                         c31[:, 64 * c + 32 * h:64 * c + 32 * (h + 1)],
                         E["R1"][:, 256 * c + 128:256 * c + 192],
                         start=True, stop=True, tile_position=(0, 64 * h))
                    t.op("matmul", psX[64 * h:64 * h + 16, off + 32:off + 64],
                         c15[:, 32 * c + 16 * h:32 * c + 16 * (h + 1)],
                         E["R1"][:, 256 * c + 192:256 * c + 224],
                         start=True, stop=True, tile_position=(0, 64 * h))
                    mmv = t.op("matmul",
                               psX[64 * h + 32:64 * h + 48, off + 96:off + 128],
                               c47[:, 32 * c + 16 * h:32 * c + 16 * (h + 1)],
                               E["R1"][:, 256 * c + 224:256 * c + 256],
                               start=True, stop=True,
                               tile_position=(0, 64 * h + 32),
                               sem=(S["mm"] if (si == 1 and ci == 3 and h == 1)
                                    else None))
        v.wait(S["mm"], mmv)
        for (mat, psX, msk4, toff) in (
                (E["A0m"], psA, E["maskS4"], 0), (E["prbm"], psA, E["maskI4"], 1),
                (E["akm"], psK, E["maskS4"], 0), (E["prkm"], psK, E["maskI4"], 1)):
            src_ = psX[:].rearrange("p (i g q) -> p i g q", i=4, g=8)[:, :, toff::2, :]
            v.op("tensor_copy", E["xtr"][:].rearrange(
                "p (i g q) -> p i g q", i=4, g=4), src_)
            v.op("select", E["xsel"][:], msk4[:], E["xtr"][:], E["zf32"][:, 0:256])
            for h in range(2):
                hs = slice(64 * h, 64 * h + 64)
                v.op("tensor_copy",
                     mat[hs, :].rearrange("p (c two x) -> p c two x", two=2, x=64)
                     [:, 4 * bk:4 * bk + 4, h, :],
                     E["xsel"][hs, :].rearrange("p (i q) -> p i q", i=4))
        ext_dv = v.op("engine_nop", sem=S["dv"])
        pool.release(biA, (S["dv"], ext_dv))
        pool.release(biK, (S["dv"], ext_dv))

    t.wait(S["dv"], S["dv"].n)
    for bk in range(2):
        bi, ps = pool.acquire(t)
        pb16 = ps[:].bitcast(BF16)
        mmv = None
        for ci in range(4):
            c = 4 * bk + ci
            for h2 in range(2):
                mmv = t.op("transpose",
                           pb16[64 * h2:64 * h2 + 64, 128 * ci:128 * ci + 128],
                           E["A0m"][:, 128 * c + 64 * h2:128 * c + 64 * (h2 + 1)],
                           E["id128"][:], tile_position=(0, 64 * h2),
                           sem=(S["mm"] if (ci == 3 and h2 == 1) else None))
        v.wait(S["mm"], mmv)
        dvv = v.op("tensor_copy", E["B0m"][:, 512 * bk:512 * bk + 512],
                   pb16[:, 0:512], sem=S["dv"])
        pool.release(bi, (S["dv"], dvv))

    t.wait(S["dv"], S["dv"].n)
    sq_plan = [(E["pB1"], E["A0m"], E["B0m"]), (E["pA1"], E["B0m"], E["A0m"]),
               (E["pB2"], E["pA1"], E["pB1"]), (E["pA2"], E["pB1"], E["pA1"]),
               (E["pB3"], E["pA2"], E["pB2"])]
    for lvl, (dstb, lb, rb) in enumerate(sq_plan):
        if lvl >= 2:
            t.wait(S["dv"], S["dv"].n)
        for bk in range(2):
            bi, ps = pool.acquire(t)
            mmv = None
            for ci in range(4):
                c = 4 * bk + ci
                mmv = t.op("matmul", ps[0:128, 128 * ci:128 * (ci + 1)],
                           lb[:, 128 * c:128 * (c + 1)],
                           rb[:, 128 * c:128 * (c + 1)], start=True, stop=True,
                           sem=(S["mm"] if ci == 3 else None))
            v.wait(S["mm"], mmv)
            dvv = v.op("tensor_copy", dstb[:, 512 * bk:512 * bk + 512],
                       ps[:, 0:512], sem=S["dv"])
            pool.release(bi, (S["dv"], dvv))
    dv_sq = S["dv"].n
    v.wait(S["dv"], dv_sq)
    v.op("tensor_tensor", E["Gb"][0][:].rearrange("p (c x) -> p c x", x=128),
         E["A0m"][:].rearrange("p (c x) -> p c x", x=128),
         E["id128"][:].unsqueeze(1).broadcast_to([128, CPG, 128]), ALU.add,
         sem=S["dv"])
    gsrc = 0
    for pb_ in (E["pB1"], E["pB2"], E["pB3"]):
        gdst = 1 - gsrc
        t.wait(S["dv"], S["dv"].n)
        for bk in range(2):
            bi, ps = pool.acquire(t)
            mmv = None
            for ci in range(4):
                c = 4 * bk + ci
                mmv = t.op("matmul", ps[0:128, 128 * ci:128 * (ci + 1)],
                           pb_[:, 128 * c:128 * (c + 1)],
                           E["Gb"][gsrc][:, 128 * c:128 * (c + 1)],
                           start=True, stop=True,
                           sem=(S["mm"] if ci == 3 else None))
            v.wait(S["mm"], mmv)
            dvv = v.op("tensor_add", E["Gb"][gdst][:, 512 * bk:512 * bk + 512],
                       ps[:, 0:512], E["Gb"][gsrc][:, 512 * bk:512 * bk + 512],
                       sem=S["dv"])
            pool.release(bi, (S["dv"], dvv))
        gsrc = gdst
    Gf = E["Gb"][gsrc]

    t.wait(S["dv"], S["dv"].n)
    for bk in range(2):
        bi, ps = pool.acquire(t)
        mmv = None
        for ci in range(4):
            c = 4 * bk + ci
            mmv = t.op("matmul", ps[0:128, 128 * ci:128 * (ci + 1)],
                       E["akm"][:, 128 * c:128 * (c + 1)],
                       E["vt2"][:, 128 * c:128 * (c + 1)],
                       start=True, stop=True, sem=(S["mm"] if ci == 3 else None))
        v.wait(S["mm"], mmv)
        dvv = v.op("tensor_copy", E["mkv"][:, 512 * bk:512 * bk + 512],
                   ps[:, 0:512], sem=S["dv"])
        pool.release(bi, (S["dv"], dvv))
    t.wait(S["dv"], S["dv"].n)
    for qb in range(4):
        bi, ps = pool.acquire(t)
        mmv = None
        for ci in range(2):
            c = 2 * qb + ci
            off = 256 * ci
            t.op("matmul", ps[0:128, off:off + 128],
                 Gf[:, 128 * c:128 * (c + 1)],
                 E["abar2"][:, 128 * c:128 * (c + 1)], start=True, stop=True)
            mmv = t.op("matmul", ps[0:128, off + 128:off + 256],
                       Gf[:, 128 * c:128 * (c + 1)],
                       E["mkv"][:, 128 * c:128 * (c + 1)], start=True, stop=True,
                       sem=(S["mm"] if ci == 1 else None))
        v.wait(S["mm"], mmv)
        for ci in range(2):
            c = 2 * qb + ci
            off = 256 * ci
            v.op("select", E["Xb"][:, 256 * c:256 * c + 128], E["maskBD"][:],
                 ps[:, off:off + 128], E["zf32"][:, 0:128])
            v.op("tensor_copy", E["Xb"][:, 256 * c + 128:256 * c + 256],
                 ps[:, off + 128:off + 256])
        dvv = v.op("engine_nop", sem=S["dv"])
        pool.release(bi, (S["dv"], dvv))
    t.wait(S["dv"], S["dv"].n)
    xa = lambda c: E["Xb"][:, 256 * c:256 * c + 128]
    hloc = lambda c: E["Xb"][:, 256 * c + 128:256 * c + 256]
    for qb in range(4):
        bi, ps = pool.acquire(t)
        mmv = None
        for ci in range(2):
            c = 2 * qb + ci
            off = 256 * ci
            t.op("matmul", ps[0:128, off:off + 128], xa(c),
                 E["bhat2"][:, 128 * c:128 * (c + 1)], start=True, stop=True)
            mmv = t.op("matmul", ps[0:128, off + 128:off + 256], xa(c),
                       E["prbm"][:, 128 * c:128 * (c + 1)], start=True, stop=True,
                       sem=(S["mm"] if ci == 1 else None))
        v.wait(S["mm"], mmv)
        for ci in range(2):
            c = 2 * qb + ci
            off = 256 * ci
            v.op("select", E["xtr"][:, 0:128], E["maskBD"][:],
                 ps[:, off:off + 128], E["zf32"][:, 0:128])
            v.op("tensor_add", E["Abd"][:, 128 * c:128 * (c + 1)],
                 E["xtr"][:, 0:128], E["Db"][:, 128 * c:128 * (c + 1)])
            v.op("select", E["xtr"][:, 128:256], E["maskBD"][:],
                 ps[:, off + 128:off + 256], E["zf32"][:, 0:128])
            for h in range(2):
                hs = slice(64 * h, 64 * h + 64)
                v.op("tensor_add",
                     E["RfBD"][hs, 128 * c + 64 * h:128 * c + 64 * (h + 1)],
                     E["xtr"][hs, 128 + 64 * h:128 + 64 * (h + 1)],
                     E["rbar"][hs, 64 * c:64 * (c + 1)])
        dvv = v.op("engine_nop", sem=S["dv"])
        pool.release(bi, (S["dv"], dvv))
    t.wait(S["dv"], S["dv"].n)
    for bk in range(2):
        bi, ps = pool.acquire(t)
        mmv = None
        for ci in range(4):
            c = 4 * bk + ci
            t.op("matmul", ps[0:128, 128 * ci:128 * ci + 128],
                 E["khatZ"][:, 128 * c:128 * (c + 1)],
                 E["vt2"][:, 128 * c:128 * (c + 1)], start=True, stop=False)
            mmv = t.op("matmul", ps[0:128, 128 * ci:128 * ci + 128],
                       E["bhatBD"][:, 128 * c:128 * (c + 1)], hloc(c),
                       start=False, stop=True, sem=(S["mm"] if ci == 3 else None))
        v.wait(S["mm"], mmv)
        for h in range(2):
            hs = slice(64 * h, 64 * h + 64)
            v.op("tensor_copy",
                 E["Ub"][hs, 256 * bk:256 * bk + 256]
                 .rearrange("p (c x) -> p c x", x=64),
                 ps[hs, :].rearrange("p (c two x) -> p c two x", two=2, x=64)
                 [:, :, h, :])
        dvv = v.op("engine_nop", sem=S["dv"])
        pool.release(bi, (S["dv"], dvv))

    t.wait(S["dv"], S["dv"].n)
    for bk in range(2):
        bi, ps = pool.acquire(t)
        mmv = None
        for ci in range(4):
            c = 4 * bk + ci
            t.op("matmul", ps[0:128, 128 * ci:128 * ci + 128],
                 E["prkm"][:, 128 * c:128 * (c + 1)],
                 E["vt2"][:, 128 * c:128 * (c + 1)], start=True, stop=False)
            mmv = t.op("matmul", ps[0:128, 128 * ci:128 * ci + 128],
                       E["prbm"][:, 128 * c:128 * (c + 1)], hloc(c),
                       start=False, stop=True, sem=(S["mm"] if ci == 3 else None))
        v.wait(S["mm"], mmv)
        for h in range(2):
            hs = slice(64 * h, 64 * h + 64)
            v.op("tensor_copy",
                 E["ylocb"][hs, 256 * bk:256 * bk + 256]
                 .rearrange("p (c x) -> p c x", x=64),
                 ps[hs, :].rearrange("p (c two x) -> p c two x", two=2, x=64)
                 [:, :, h, :])
        dvv = v.op("engine_nop", sem=S["dv"])
        pool.release(bi, (S["dv"], dvv))
    t.wait(S["dv"], S["dv"].n)
    bi_y5, ps_y5 = pool.acquire(t)
    bi_st, ps_st = pool.acquire(t)
    for c in range(CPG):
        t.op("matmul", ps_y5[0:128, 0:64], E["RfBD"][:, 128 * c:128 * (c + 1)],
             E["state"][:], start=True, stop=True)
        mm_y = t.op("matmul", ps_st[0:128, 64:128],
                    E["Abd"][:, 128 * c:128 * (c + 1)], E["state"][:],
                    start=True, stop=True, sem=S["mm"])
        v.wait(S["mm"], mm_y)
        v.op("tensor_add", E["state"][:], ps_st[0:128, 64:128],
             E["Ub"][:, 64 * c:64 * c + 64])
        dv_yc = v.op("tensor_add", E["ycg"][:, 64 * c:64 * c + 64],
                     ps_y5[0:128, 0:64], E["ylocb"][:, 64 * c:64 * c + 64],
                     sem=S["dv"])
        t.wait(S["dv"], dv_yc)
    dv_y = S["dv"].n
    pool.release(bi_st, (S["dv"], dv_y))
    pool.release(bi_y5, (S["dv"], dv_y))

    g.wait(S["dv"], dv_y)
    for h in range(2):
        g.dma(E["SC"]["y"][E["b"], E["t0"]:E["t0"] + 512,
                           E["c0"] + 64 * h:E["c0"] + 64 * (h + 1)]
              .rearrange("(c p) x -> p c x", p=64),
              E["ycg"][64 * h:64 * h + 64, :].rearrange("p (c x) -> p c x", x=64),
              S["stv"])
    g.wait(S["stv"], S["stv"].n)
    barrier(P, S)


def plan_stage_c(nc, P, S, W, SC, CONST, pool, st):
    import contextlib
    stage = contextlib.ExitStack()

    def sb(nm, sh, dt=BF16):
        return stage.enter_context(nc.sbuf_tensor(nm, sh, dt))

    g, sy, sc, v, t = P.gpsimd, P.sync, P.scalar, P.vector, P.tensor
    SD = CONST["SD"]

    e0i = sb("c_e0i", [128, 128], I32)
    e0 = sb("c_e0", [128, 128])
    stg = sb("c_stg", [128, 1024])
    bc_lnw = sb("c_lnw", [128, 1024])
    bc_lnb = sb("c_lnb", [128, 1024])
    bc_fa = sb("c_fa", [128, 1024])
    wot = sb("c_wot", [128, 8192])
    g.op("iota", e0i[:], [[0, 128]], channel_multiplier=1, sem=S["io"])
    v.wait(S["io"], S["io"].n)
    v.op("tensor_single_scalar", e0[:], e0i[:], 0, ALU.is_equal)
    v.op("memset", stg[:], 0.0, sem=S["dv"])
    g.wait(S["dv"], S["dv"].n)
    for icb in range(8):
        sy.dma(wot[:, 1024 * icb:1024 * (icb + 1)],
               SC["wo_bf"][0:C, 128 * icb:128 * (icb + 1)], S["ld2"], transpose=True)
    wot_ld = S["ld2"].n
    vec_srcs = [(bc_lnw, W["attn_ln_w"]), (bc_lnb, W["attn_ln_b"]),
                (bc_fa, W["attn_time_faaaa"].rearrange("a b h n -> (a b) (h n)")[0])]
    for vi, (dstb, src) in enumerate(vec_srcs):
        if vi > 0:
            g.wait(S["mm"], S["mm"].n)
        swpv = g.dma(stg[0:1, :], src.unsqueeze(0), S["swp"])
        t.wait(S["swp"], swpv)
        for half in range(2):
            bi, ps = pool.acquire(t)
            mmv = t.op("matmul", ps[0:128, 0:512], e0[:],
                       stg[:, 512 * half:512 * (half + 1)], start=True, stop=True,
                       sem=S["mm"])
            v.wait(S["mm"], mmv)
            dvv = v.op("tensor_copy", dstb[:, 512 * half:512 * (half + 1)],
                       ps[:, 0:512], sem=S["dv"])
            pool.release(bi, (S["dv"], dvv))
    v.wait(S["dv"], S["dv"].n)
    sc.wait(S["dv"], S["dv"].n)
    sc.op("activation", stg[:, 0:3], bc_lnw[:, 0:3], AF.Square)
    sc.op("activation", stg[:, 3:6], bc_lnb[:, 0:3], AF.Square)
    sc.op("activation", stg[:, 6:9], bc_fa[:, 0:3], AF.Square, sem=S["ac"])
    v.wait(S["ac"], S["ac"].n)

    yb = [sb(f"c_y{p}", [128, 1024]) for p in range(2)]
    rb = [sb(f"c_r{p}", [128, 1024]) for p in range(2)]
    kb = [sb(f"c_k{p}", [128, 1024]) for p in range(2)]
    vb = [sb(f"c_v{p}", [128, 1024]) for p in range(2)]
    gb = [sb(f"c_g{p}", [128, 1024]) for p in range(2)]
    xb = [sb(f"c_x{p}", [128, 1024], F32) for p in range(2)]
    xz = [sb(f"c_xz{p}", [128, 1024], F32) for p in range(2)]
    x1bb = [sb(f"c_x1b{p}", [128, 1024]) for p in range(2)]
    tmp1 = sb("c_t1", [128, 1024], F32)
    tmp2 = sb("c_t2", [128, 1024], F32)
    ygb = sb("c_yg", [128, 1024])
    ygT = sb("c_ygT", [128, 1024])
    st16 = sb("c_st16", [128, 16 * 6], F32)
    red_y = st16[:, 0:16]; red_y2 = st16[:, 16:32]; mean = st16[:, 32:48]
    m2 = st16[:, 48:64]; var16 = st16[:, 64:80]; ivar = st16[:, 80:96]
    st1 = sb("c_st1", [128, 8], F32)
    r1 = st1[:, 0:1]; r2 = st1[:, 1:2]; mm_ = st1[:, 2:3]; mm2 = st1[:, 3:4]
    varL = st1[:, 4:5]; invL = st1[:, 5:6]

    ldsem = [S["rs14"], S["rs15"]]
    stsem = [S["rs4"], S["rs5"]]
    done_dv = {}
    done_st = {}
    r3 = lambda bb: bb[:].rearrange("p (h x) -> p h x", h=16)
    bc16 = lambda apc: apc.unsqueeze(2).broadcast_to([128, 16, 64])
    bc1k = lambda apc: apc.broadcast_to([128, 1024])

    if DEBUG_TAPS:
        dbg_c = nc.dram_tensor("s_dbgc", [4, 5, 128, 1024], BF16,
                               kind="ExternalOutput").ap()
        dbg_cx = nc.dram_tensor("s_dbgcx", [4, 128, 1024], F32,
                                kind="ExternalOutput").ap()
        dbg_xz = nc.dram_tensor("s_dbgxz", [4, 128, 1024], F32,
                                kind="ExternalOutput").ap()
    for idx, (b, tk) in enumerate([(b, tk) for b in range(B) for tk in range(T // 128)]):
        p = idx % 2
        t0 = 128 * tk
        if idx >= 2:
            sy.wait(S["dv"], done_dv[p])
            sy.wait(stsem[p], done_st[p])
        sy.dma(yb[p][:], SC["y"][b, t0:t0 + 128, :], ldsem[p])
        sy.dma(rb[p][:], SD["r"][b, t0:t0 + 128, :], ldsem[p])
        sy.dma(kb[p][:], SD["k"][b, t0:t0 + 128, :], ldsem[p])
        sy.dma(vb[p][:], SD["v"][b, t0:t0 + 128, :], ldsem[p])
        sy.dma(gb[p][:], SD["g"][b, t0:t0 + 128, :], ldsem[p])
        sy.dma(xb[p][:], SC["xin"][b, t0:t0 + 128, :], ldsem[p])
        ldv = ldsem[p].n
        if DEBUG_TAPS and idx < 4:
            g.wait(ldsem[p], ldv)
            for bi_, bufd in enumerate((yb[p], rb[p], kb[p], vb[p], gb[p])):
                g.dma(dbg_c[idx, bi_], bufd[:], S["dbg"])
            g.dma(dbg_cx[idx], xb[p][:], S["dbg"])
            g.wait(S["dbg"], S["dbg"].n)
        v.wait(ldsem[p], ldv)

        def vo(meth, *a, **kw):
            val = v.op(meth, *a, sem=S["dv"], **kw)
            if idx < 8:
                v.wait(S["dv"], val)
            return val

        def so(meth, *a, **kw):
            val = sc.op(meth, *a, sem=S["ac"], **kw)
            if idx < 8:
                sc.wait(S["ac"], val)
            return val
        vo("tensor_tensor", tmp2[:], yb[p][:], yb[p][:], ALU.mult)
        vo("tensor_reduce", red_y[:].unsqueeze(2), r3(yb[p]), AX.X, ALU.add)
        vo("tensor_reduce", red_y2[:].unsqueeze(2), r3(tmp2), AX.X, ALU.add)
        vo("tensor_scalar_mul", mean[:], red_y[:], 1.0 / 64)
        vo("tensor_tensor", m2[:], mean[:], mean[:], ALU.mult)
        vo("scalar_tensor_tensor", var16[:], red_y2[:], 1.0 / 64, m2[:],
           ALU.mult, ALU.subtract)
        sc.wait(S["dv"], S["dv"].n)
        so("activation", var16[:], var16[:], AF.Ln, bias=0.00064)
        so("activation", ivar[:], var16[:], AF.Exp, scale=-0.5)
        v.wait(S["ac"], S["ac"].n)
        vo("tensor_tensor", r3(tmp1), r3(yb[p]), bc16(mean[:]), ALU.subtract)
        vo("tensor_tensor", r3(tmp1), r3(tmp1), bc16(ivar[:]), ALU.mult)
        vo("tensor_tensor", tmp1[:], tmp1[:], bc_lnw[:], ALU.mult)
        vo("tensor_tensor", tmp1[:], tmp1[:], bc_lnb[:], ALU.add)
        vo("tensor_tensor", tmp2[:], rb[p][:], kb[p][:], ALU.mult)
        vo("tensor_tensor", tmp2[:], tmp2[:], bc_fa[:], ALU.mult)
        vo("tensor_reduce", red_y[:].unsqueeze(2), r3(tmp2), AX.X, ALU.add)
        vo("tensor_tensor", r3(tmp2), r3(vb[p]), bc16(red_y[:]), ALU.mult)
        vo("tensor_tensor", tmp1[:], tmp1[:], tmp2[:], ALU.add)
        dv_yg = vo("tensor_tensor", ygb[:], tmp1[:], gb[p][:], ALU.mult)
        t.wait(S["dv"], dv_yg)
        for bk in range(2):
            bi, ps = pool.acquire(t)
            pb16 = ps[:].bitcast(BF16)
            mmv = None
            for ci in range(4):
                cb = 4 * bk + ci
                for hf in range(2):
                    mmv = t.op("transpose",
                               pb16[64 * hf:64 * hf + 64, 128 * ci:128 * ci + 128],
                               ygb[:, 128 * cb + 64 * hf:128 * cb + 64 * (hf + 1)],
                               CONST["id128c"][:], tile_position=(0, 64 * hf),
                               sem=(S["mm"] if (ci == 3 and hf == 1) else None))
            v.wait(S["mm"], mmv)
            dvv = vo("tensor_copy", ygT[:, 512 * bk:512 * bk + 512],
                     pb16[:, 0:512])
            pool.release(bi, (S["dv"], dvv))
        t.wait(S["dv"], S["dv"].n)
        if idx == 0:
            t.wait(S["ld2"], wot_ld)
        for half in range(2):
            bi, ps = pool.acquire(t)
            mmv = None
            for icb in range(8):
                mmv = t.op("matmul", ps[0:128, 0:512],
                           ygT[:, 128 * icb:128 * (icb + 1)],
                           wot[:, 1024 * icb + 512 * half:1024 * icb + 512 * (half + 1)],
                           start=(icb == 0), stop=(icb == 7),
                           sem=(S["mm"] if icb == 7 else None))
            v.wait(S["mm"], mmv)
            dvv = vo("tensor_add", xz[p][:, 512 * half:512 * (half + 1)],
                     ps[:, 0:512], xb[p][:, 512 * half:512 * (half + 1)])
            pool.release(bi, (S["dv"], dvv))
        if DEBUG_TAPS and idx < 4:
            g.wait(S["dv"], dvv)
            g.dma(dbg_xz[idx], xz[p][:], S["dbg"])
            g.wait(S["dbg"], S["dbg"].n)
        vo("tensor_tensor", tmp2[:], xz[p][:], xz[p][:], ALU.mult)
        vo("tensor_reduce", r1, xz[p][:], AX.X, ALU.add)
        vo("tensor_reduce", r2, tmp2[:], AX.X, ALU.add)
        vo("tensor_scalar_mul", mm_, r1, 1.0 / 1024)
        vo("tensor_tensor", mm2, mm_, mm_, ALU.mult)
        vo("scalar_tensor_tensor", varL, r2, 1.0 / 1024, mm2,
           ALU.mult, ALU.subtract)
        sc.wait(S["dv"], S["dv"].n)
        so("activation", varL, varL, AF.Ln, bias=1e-5)
        so("activation", invL, varL, AF.Exp, scale=-0.5)
        v.wait(S["ac"], S["ac"].n)
        vo("tensor_tensor", xz[p][:], xz[p][:], bc1k(mm_), ALU.subtract)
        vo("tensor_tensor", xz[p][:], xz[p][:], bc1k(invL), ALU.mult)
        dvf = vo("tensor_copy", x1bb[p][:], xz[p][:])
        done_dv[p] = dvf
        g.wait(S["dv"], dvf)
        g.dma(SC["x1"][b, TG + t0:TG + t0 + 128, :], xz[p][:], stsem[p])
        g.dma(SC["x1b"][b, TG + t0:TG + t0 + 128, :], x1bb[p][:], stsem[p])
        done_st[p] = stsem[p].n
    g.wait(S["rs4"], S["rs4"].n)
    g.wait(S["rs5"], S["rs5"].n)
    barrier(P, S)
    stage.close()


def plan_ffn_attn(nc, P, S, W, SC, CONST, pool, st, layer):
    import contextlib
    stage = contextlib.ExitStack()

    def sb(nm, sh, dt=BF16):
        return stage.enter_context(nc.sbuf_tensor(f"fa{layer}_" + nm, sh, dt))

    g, sy, sc, v, t = P.gpsimd, P.sync, P.scalar, P.vector, P.tensor
    srcf = SC["x1"] if layer == 0 else SC["x3"]
    srcb = SC["x1b"] if layer == 0 else SC["x3b"]
    dstf, dstb = SC["x2"], SC["x2b"]
    id128 = CONST["id128c"]
    zeros = CONST["zeros"]
    WIN, NW = 512, T // 512

    ii = sb("ii", [128, 128], I32)
    i0_ = sb("i0", [128, 128], I32)
    mge = sb("mge", [128, 128], U8)
    mle = sb("mle", [128, 128], U8)
    ninf = sb("ninf", [128, 128], F32)
    onesc = sb("ones", [128, 1])
    e0 = sb("e0", [128, 128])
    g.op("iota", ii[:], [[-1, 128]], channel_multiplier=1)
    g.op("iota", i0_[:], [[0, 128]], channel_multiplier=1, sem=S["io"])
    v.wait(S["io"], S["io"].n)
    v.op("tensor_single_scalar", mge[:], ii[:], 0, ALU.is_ge)
    v.op("tensor_single_scalar", mle[:], ii[:], 0, ALU.is_le)
    v.op("tensor_single_scalar", e0[:], i0_[:], 0, ALU.is_equal)
    v.op("memset", ninf[:], -1e30)
    v.op("memset", onesc[:], 1.0)

    wq_sb = sb("wq", [128, 12288])
    wo_sb = sb("wo", [128, 4096])
    stg = sb("stg", [128, 512])
    bc_qs = sb("bcqs", [128, 512])
    bc_ks = sb("bcks", [128, 512])
    for icb in range(8):
        g.dma(wq_sb[:, 1536 * icb:1536 * (icb + 1)],
              W["ffn_wqkv"][layer, 128 * icb:128 * (icb + 1), :], S["swp"])
    for ib in range(4):
        g.dma(wo_sb[:, 1024 * ib:1024 * (ib + 1)],
              W["ffn_wo"][layer, 128 * ib:128 * (ib + 1), :], S["swp"])
    dvz = v.op("memset", stg[:], 0.0, sem=S["dv"])
    g.wait(S["dv"], dvz)
    for vi, (dstb_, wkey) in enumerate(((bc_qs, "ffn_q_scale"), (bc_ks, "ffn_k_scale"))):
        if vi > 0:
            g.wait(S["mm"], S["mm"].n)
        for hh in range(8):
            g.dma(stg[0:1, 64 * hh:64 * (hh + 1)],
                  W[wkey][layer].unsqueeze(0), S["swp"])
        swpv = S["swp"].n
        t.wait(S["swp"], swpv)
        bi, ps = pool.acquire(t)
        mmv = t.op("matmul", ps[0:128, 0:512], e0[:], stg[:], start=True, stop=True,
                   sem=S["mm"])
        v.wait(S["mm"], mmv)
        dvv = v.op("tensor_copy", dstb_[:], ps[:, 0:512], sem=S["dv"])
        pool.release(bi, (S["dv"], dvv))

    hT = sb("hT", [128, 1024])
    qS = sb("qS", [128, 2048])
    kS = sb("kS", [128, 4096])
    vS = sb("vS", [128, 4096])
    qZm = sb("qZm", [128, 4096])
    kE = sb("kE", [128, 8192])
    pS = sb("pS", [128, 4096])
    ob = sb("ob", [128, 2048])
    oT = sb("oT", [128, 512])
    xres = sb("xres", [128, 1024], F32)
    xzf = sb("xzf", [128, 1024], F32)
    xzb = sb("xzb", [128, 1024])
    tq = sb("tq", [128, 512], F32)
    tk = sb("tk", [128, 512], F32)
    tn = sb("tn", [128, 512], F32)
    qrot = sb("qrot", [128, 512])
    krot = sb("krot", [128, 512])
    mcs = sb("mcs", [128, 64], F32); mss = sb("mss", [128, 64], F32)
    mkc = sb("mkc", [128, 64], F32); mks = sb("mks", [128, 64], F32)
    st8 = sb("st8", [128, 32], F32)
    mcol = sb("mcol", [128, 4], F32)
    mb16 = sb("mb16", [128, 4])
    dinv = sb("dinv", [128, 4], F32)
    mtf = sb("mtf", [128, 128], F32)
    mtb = sb("mtb", [128, 128])
    dvv = v.op("memset", qZm[:], 0.0)
    dvv = v.op("memset", kE[:], 0.0, sem=S["dv"])
    v.wait(S["dv"], S["dv"].n)
    for h in range(8):
        gq = 64 * (1 - h % 2)
        v.op("memset", kE[gq:gq + 1, 1024 * h:1024 * (h + 1)], 1.0)
    dv_init = v.op("engine_nop", sem=S["dv"])
    t.wait(S["dv"], dv_init)
    t.wait(S["swp"], S["swp"].n)

    r8 = lambda bb: bb[:].rearrange("p (h x) -> p h x", h=8)
    bc8 = lambda apc: apc.unsqueeze(2).broadcast_to([128, 8, 64])
    bcm = lambda apc: apc.unsqueeze(1).broadcast_to([128, 8, 64])
    bcm2 = lambda apc: apc.unsqueeze(1).broadcast_to([128, 8, 32])
    hT_mm = hT_dv = 0
    xres_dv = 0
    if DEBUG_TAPS and layer == 0:
        dbg_fa = nc.dram_tensor("s_dbgfa", [9, 128, 4096], BF16,
                                kind="ExternalOutput").ap()

    for b in range(B):
        for w in range(NW):
            slot, pslot = w % 2, (w - 1) % 2

            ser = True

            def vq(meth, *a, _ser=None, **kw):
                val = v.op(meth, *a, sem=S["dv"], **kw)
                if ser if _ser is None else _ser:
                    v.wait(S["dv"], val)
                return val

            def sq(meth, *a, _ser=None, **kw):
                val = sc.op(meth, *a, sem=S["ac"], **kw)
                if ser if _ser is None else _ser:
                    sc.wait(S["ac"], val)
                return val
            for ck in range(4):
                t0 = WIN * w + 128 * ck
                sy.wait(S["mm"], hT_mm)
                sy.wait(S["dv"], hT_dv)
                for icb in range(8):
                    roff = TG + t0 - (1 if icb >= 4 else 0)
                    sy.dma(hT[:, 128 * icb:128 * (icb + 1)],
                           srcb[b, roff:roff + 128, 128 * icb:128 * (icb + 1)],
                           S["ld"], transpose=True)
                sy.dma(mcs[:], W["ffn_rot_qc"][t0:t0 + 128, :], S["ld"])
                sy.dma(mss[:], W["ffn_rot_qs"][t0:t0 + 128, :], S["ld"])
                sy.dma(mkc[:], W["ffn_rot_kc"][t0:t0 + 128, :], S["ld"])
                sy.dma(mks[:], W["ffn_rot_ks"][t0:t0 + 128, :], S["ld"])
                ldv = S["ld"].n
                t.wait(S["ld"], ldv)
                v.wait(S["ld"], ldv)
                banks = []
                for part in range(3):
                    bi, ps = pool.acquire(t)
                    mmv = None
                    for icb in range(8):
                        mmv = t.op("matmul", ps[0:128, 0:512],
                                   hT[:, 128 * icb:128 * (icb + 1)],
                                   wq_sb[:, 1536 * icb + 512 * part:
                                         1536 * icb + 512 * (part + 1)],
                                   start=(icb == 0), stop=(icb == 7),
                                   sem=(S["mm"] if icb == 7 else None))
                    banks.append((bi, ps, mmv))
                hT_mm = banks[2][2]
                v.wait(S["mm"], banks[2][2])
                vq("tensor_copy", vS[:, 2048 * slot + 512 * ck:2048 * slot + 512 * (ck + 1)],
                   banks[2][1][:, 0:512])
                for (tag, bnk, bcsc, mc, ms, dst) in (
                        ("q", banks[0], bc_qs, mcs, mss, qrot),
                        ("k", banks[1], bc_ks, mkc, mks, krot)):
                    _, ps, mmv = bnk
                    v.wait(S["mm"], mmv)
                    nrm = st8[:, 0:8]
                    vq("tensor_copy", tn[:], ps[:, 0:512])
                    vq("tensor_tensor", tq[:], tn[:], tn[:], ALU.mult)
                    vq("tensor_reduce", nrm[:].unsqueeze(2), r8(tq), AX.X, ALU.add)
                    vq("tensor_single_scalar", nrm[:], nrm[:], 1e-24, ALU.max)
                    sc.wait(S["dv"], S["dv"].n)
                    sq("activation", nrm[:], nrm[:], AF.Ln)
                    sq("activation", nrm[:], nrm[:], AF.Exp, scale=-0.5)
                    v.wait(S["ac"], S["ac"].n)
                    vq("tensor_tensor", r8(tn), r8(tn), bc8(nrm[:]),
                       ALU.mult)
                    vq("tensor_tensor", tn[:], tn[:], bcsc[:], ALU.mult)
                    vq("tensor_tensor", r8(tq), r8(tn), bcm(mc[:]), ALU.mult)
                    vq("tensor_tensor", r8(tk)[:, :, 0:32], r8(tn)[:, :, 32:64],
                       bcm2(ms[:, 0:32]), ALU.mult)
                    vq("tensor_tensor", r8(tk)[:, :, 32:64], r8(tn)[:, :, 0:32],
                       bcm2(ms[:, 32:64]), ALU.mult)
                    vq("tensor_tensor", r8(dst)[:, :, 0:32], r8(tq)[:, :, 0:32],
                       r8(tk)[:, :, 0:32], ALU.subtract)
                    vq("tensor_tensor", r8(dst)[:, :, 32:64], r8(tq)[:, :, 32:64],
                       r8(tk)[:, :, 32:64], ALU.add)
                dv_rot = v.op("engine_nop", sem=S["dv"])
                hT_dv = dv_rot
                for bi, _, _ in banks:
                    pool.release(bi, (S["dv"], dv_rot))
                t.wait(S["dv"], dv_rot)
                bi, ps = pool.acquire(t)
                pb16 = ps[:].bitcast(BF16)
                mmv = None
                for hp in range(4):
                    for hf in range(2):
                        mmv = t.op("transpose",
                                   pb16[64 * hf:64 * hf + 64, 128 * hp:128 * (hp + 1)],
                                   qrot[:, 128 * hp + 64 * hf:128 * hp + 64 * (hf + 1)],
                                   id128[:], tile_position=(0, 64 * hf))
                for hp in range(4):
                    for hf in range(2):
                        mmv = t.op("transpose",
                                   pb16[64 * hf:64 * hf + 64,
                                        512 + 128 * hp:512 + 128 * (hp + 1)],
                                   krot[:, 128 * hp + 64 * hf:128 * hp + 64 * (hf + 1)],
                                   id128[:], tile_position=(0, 64 * hf),
                                   sem=(S["mm"] if (hp == 3 and hf == 1) else None))
                v.wait(S["mm"], mmv)
                for hp in range(4):
                    vq("tensor_copy", qS[:, 512 * hp + 128 * ck:512 * hp + 128 * (ck + 1)],
                       pb16[:, 128 * hp:128 * (hp + 1)])
                    vq("tensor_copy",
                       kS[:, 2048 * slot + 512 * hp + 128 * ck:
                          2048 * slot + 512 * hp + 128 * (ck + 1)],
                       pb16[:, 512 + 128 * hp:512 + 128 * (hp + 1)])
                dvv = v.op("engine_nop", sem=S["dv"])
                pool.release(bi, (S["dv"], dvv))
            if DEBUG_TAPS and b == 0 and w == 0 and layer == 0:
                g.wait(S["dv"], S["dv"].n)
                g.dma(dbg_fa[0][:, 0:2048], qS[:], S["dbg"])
                g.dma(dbg_fa[1][:, 0:2048], kS[:, 2048 * slot:2048 * (slot + 1)], S["dbg"])
                g.dma(dbg_fa[2][:, 0:2048], vS[:, 2048 * slot:2048 * (slot + 1)], S["dbg"])
                g.dma(dbg_fa[7][:, 0:512], bc_qs[:], S["dbg"])
                g.dma(dbg_fa[7][:, 512:1024], bc_ks[:], S["dbg"])
                g.dma(dbg_fa[8][:, 0:512], qrot[:], S["dbg"])
                g.dma(dbg_fa[8][:, 512:1024], krot[:], S["dbg"])
                g.wait(S["dbg"], S["dbg"].n)
            for h in range(8):
                hp = h // 2
                gp = 64 * (h % 2)
                gq = 64 * (1 - h % 2)
                qz = qZm[:, 512 * h:512 * (h + 1)]
                ke = kE[:, 1024 * h:1024 * (h + 1)]
                vq("memset", qz[gq:gq + 1, :], 0.0)
                vq("tensor_copy", qz[gp:gp + 64, :],
                   qS[gp:gp + 64, 512 * hp:512 * (hp + 1)])
                dv_qz = vq("tensor_copy", ke[gp:gp + 64, 512 * slot:512 * (slot + 1)],
                           kS[gp:gp + 64, 2048 * slot + 512 * hp:
                              2048 * slot + 512 * (hp + 1)])
                t.wait(S["dv"], dv_qz)
                for ib in range(4):
                    bi, ps = pool.acquire(t)
                    mmv = t.op("matmul", ps[0:128, 0:512],
                               qz[:, 128 * ib:128 * (ib + 1)],
                               kS[:, 2048 * slot + 512 * hp:2048 * slot + 512 * (hp + 1)],
                               start=True, stop=True, sem=S["mm"])
                    v.wait(S["mm"], mmv)
                    vq("select", mtf[:], mge[:],
                       ps[:, 128 * ib:128 * (ib + 1)], ninf[:])
                    vq("tensor_reduce", mcol[:, ib:ib + 1], mtf[:], AX.X, ALU.max)
                    if ib > 0:
                        vq("tensor_reduce", st8[:, 8:9],
                           ps[:, 0:128 * ib], AX.X, ALU.max)
                        vq("tensor_tensor", mcol[:, ib:ib + 1],
                           mcol[:, ib:ib + 1], st8[:, 8:9], ALU.max)
                    dvv = v.op("engine_nop", sem=S["dv"])
                    if w > 0:
                        t.wait(S["dv"], dvv)
                        mmv = t.op("matmul", ps[0:128, 0:512],
                                   qz[:, 128 * ib:128 * (ib + 1)],
                                   kS[:, 2048 * pslot + 512 * hp:
                                      2048 * pslot + 512 * (hp + 1)],
                                   start=True, stop=True, sem=S["mm"])
                        v.wait(S["mm"], mmv)
                        vq("select", mtf[:], mle[:],
                           ps[:, 128 * ib:128 * (ib + 1)], ninf[:])
                        vq("tensor_reduce", st8[:, 8:9], mtf[:], AX.X, ALU.max)
                        vq("tensor_tensor", mcol[:, ib:ib + 1],
                           mcol[:, ib:ib + 1], st8[:, 8:9], ALU.max)
                        if ib < 3:
                            vq("tensor_reduce", st8[:, 8:9],
                               ps[:, 128 * (ib + 1):512], AX.X, ALU.max)
                            vq("tensor_tensor", mcol[:, ib:ib + 1],
                               mcol[:, ib:ib + 1], st8[:, 8:9], ALU.max)
                        dvv = v.op("engine_nop", sem=S["dv"])
                    pool.release(bi, (S["dv"], dvv))
                dv_m = vq("tensor_scalar_mul", mb16[:], mcol[:], -1.0)
                t.wait(S["dv"], dv_m)
                bi, ps = pool.acquire(t)
                mmv = None
                for ib in range(4):
                    mmv = t.op("matmul", ps[gq:gq + 1, 128 * ib:128 * (ib + 1)],
                               mb16[:, ib:ib + 1], id128[:],
                               start=True, stop=True, tile_position=(0, gq),
                               sem=(S["mm"] if ib == 3 else None))
                v.wait(S["mm"], mmv)
                dvv = vq("tensor_copy", qz[gq:gq + 1, :], ps[gq:gq + 1, 0:512])
                pool.release(bi, (S["dv"], dvv))
                t.wait(S["dv"], S["dv"].n)
                jbs = list(range(4, 8)) if w == 0 else list(range(8))
                for jb in jbs:
                    jsl = slot if jb >= 4 else pslot
                    kcol = 512 * jsl + 128 * (jb % 4)
                    bi, ps = pool.acquire(t)
                    mmv = t.op("matmul", ps[0:128, 0:512], ke[:, kcol:kcol + 128],
                               qz[:], start=True, stop=True, sem=S["mm"])
                    sc.wait(S["mm"], mmv)
                    acv = sq("activation", pS[:, 512 * jb:512 * (jb + 1)],
                             ps[0:128, 0:512], AF.Exp)
                    pool.release(bi, (S["ac"], acv))
                v.wait(S["ac"], S["ac"].n)
                for ib in range(4):
                    sl_o = slice(512 * (4 + ib) + 128 * ib, 512 * (4 + ib) + 128 * (ib + 1))
                    vq("tensor_copy", mtb[:], pS[:, sl_o])
                    vq("select", pS[:, sl_o], mle[:], mtb[:], zeros[:, 0:128])
                    if w > 0:
                        sl_p = slice(512 * ib + 128 * ib, 512 * ib + 128 * (ib + 1))
                        vq("tensor_copy", mtb[:], pS[:, sl_p])
                        vq("select", pS[:, sl_p], mge[:], mtb[:], zeros[:, 0:128])
                dv_p = v.op("engine_nop", sem=S["dv"])
                t.wait(S["dv"], dv_p)
                for ib in range(4):
                    ojbs = [j for j in jbs if (j >= 4 and j - 4 <= ib) or
                            (j < 4 and j >= ib)]
                    bi, ps = pool.acquire(t)
                    for ji, jb in enumerate(ojbs):
                        jsl = slot if jb >= 4 else pslot
                        t.op("matmul", ps[0:128, 0:64],
                             pS[:, 512 * jb + 128 * ib:512 * jb + 128 * (ib + 1)],
                             vS[:, 2048 * jsl + 512 * (jb % 4) + 64 * h:
                                2048 * jsl + 512 * (jb % 4) + 64 * (h + 1)],
                             start=(ji == 0), stop=(ji == len(ojbs) - 1))
                    for ji, jb in enumerate(ojbs):
                        mmv = t.op("matmul", ps[0:128, 64:65],
                                   pS[:, 512 * jb + 128 * ib:512 * jb + 128 * (ib + 1)],
                                   onesc[:], start=(ji == 0),
                                   stop=(ji == len(ojbs) - 1),
                                   sem=(S["mm"] if ji == len(ojbs) - 1 else None))
                    v.wait(S["mm"], mmv)
                    vq("reciprocal", dinv[:, 0:1], ps[:, 64:65])
                    dvv = vq("tensor_tensor",
                             ob[:, 512 * ib + 64 * h:512 * ib + 64 * (h + 1)],
                             ps[:, 0:64], dinv[:, 0:1].broadcast_to([128, 64]),
                             ALU.mult)
                    pool.release(bi, (S["dv"], dvv))
            if DEBUG_TAPS and b == 0 and w == 0 and layer == 0:
                g.wait(S["dv"], S["dv"].n)
                g.dma(dbg_fa[3][:, 0:2048], ob[:], S["dbg"])
                g.dma(dbg_fa[4], pS[:], S["dbg"])
                g.dma(dbg_fa[5], qZm[:], S["dbg"])
                g.dma(dbg_fa[6], kE[:, 0:4096], S["dbg"])
                g.wait(S["dbg"], S["dbg"].n)
            for ck in range(4):
                t0 = WIN * w + 128 * ck
                sy.wait(S["dv"], xres_dv)
                sy.dma(xres[:], srcf[b, TG + t0:TG + t0 + 128, :], S["ld2"])
                t.wait(S["dv"], S["dv"].n)
                bi, ps = pool.acquire(t)
                pb16 = ps[:].bitcast(BF16)
                mmv = None
                for ib in range(4):
                    for hf in range(2):
                        mmv = t.op("transpose",
                                   pb16[64 * hf:64 * hf + 64, 128 * ib:128 * (ib + 1)],
                                   ob[:, 512 * ck + 128 * ib + 64 * hf:
                                      512 * ck + 128 * ib + 64 * (hf + 1)],
                                   id128[:], tile_position=(0, 64 * hf),
                                   sem=(S["mm"] if (ib == 3 and hf == 1) else None))
                v.wait(S["mm"], mmv)
                dv_ot = vq("tensor_copy", oT[:], pb16[:, 0:512])
                pool.release(bi, (S["dv"], dv_ot))
                t.wait(S["dv"], dv_ot)
                ld2v = S["ld2"].n
                v.wait(S["ld2"], ld2v)
                for half in range(2):
                    bi, ps = pool.acquire(t)
                    mmv = None
                    for ib in range(4):
                        mmv = t.op("matmul", ps[0:128, 0:512],
                                   oT[:, 128 * ib:128 * (ib + 1)],
                                   wo_sb[:, 1024 * ib + 512 * half:
                                         1024 * ib + 512 * (half + 1)],
                                   start=(ib == 0), stop=(ib == 3),
                                   sem=(S["mm"] if ib == 3 else None))
                    v.wait(S["mm"], mmv)
                    if half == 0:
                        v.wait(S["stv"], S["stv"].n)
                    dvv = vq("tensor_add", xzf[:, 512 * half:512 * (half + 1)],
                             ps[:, 0:512], xres[:, 512 * half:512 * (half + 1)])
                    pool.release(bi, (S["dv"], dvv))
                dv_x = vq("tensor_copy", xzb[:], xzf[:])
                xres_dv = dv_x
                g.wait(S["dv"], dv_x)
                g.dma(dstf[b, TG + t0:TG + t0 + 128, :], xzf[:], S["stv"])
                g.dma(dstb[b, TG + t0:TG + t0 + 128, :], xzb[:], S["stv"])
            g.wait(S["stv"], S["stv"].n)
            barrier(P, S)
    stage.close()


def plan_ffn_mlp(nc, P, S, W, SC, CONST, pool, st, layer):
    import contextlib
    stage = contextlib.ExitStack()

    def sb(nm, sh, dt=BF16):
        return stage.enter_context(nc.sbuf_tensor(f"fm{layer}_" + nm, sh, dt))

    g, sy, sc, v, t = P.gpsimd, P.sync, P.scalar, P.vector, P.tensor
    srcf, srcb = SC["x2"], SC["x2b"]
    dstf, dstb = SC["x3"], SC["x3b"]
    id128 = CONST["id128c"]
    NJ = 22
    TTK = 256

    i0_ = sb("i0", [128, 128], I32)
    e0 = sb("e0", [128, 128])
    stg = sb("stg", [128, 1024])
    bc_lg = sb("bclg", [128, 1024])
    bc_lb = sb("bclb", [128, 1024])
    bc_b2 = sb("bcb2", [128, 1024])
    g.op("iota", i0_[:], [[0, 128]], channel_multiplier=1, sem=S["io"])
    v.wait(S["io"], S["io"].n)
    v.op("tensor_single_scalar", e0[:], i0_[:], 0, ALU.is_equal)
    dvz = v.op("memset", stg[:], 0.0, sem=S["dv"])
    g.wait(S["dv"], dvz)
    for vi, (dstb_, src) in enumerate(((bc_lg, W["ffn_ln_g"][layer]),
                                       (bc_lb, W["ffn_ln_b"][layer]),
                                       (bc_b2, W["ffn_b2"][layer]))):
        if vi > 0:
            g.wait(S["mm"], S["mm"].n)
        swpv = g.dma(stg[0:1, :], src.unsqueeze(0), S["swp"])
        t.wait(S["swp"], swpv)
        for half in range(2):
            bi, ps = pool.acquire(t)
            mmv = t.op("matmul", ps[0:128, 0:512], e0[:],
                       stg[:, 512 * half:512 * (half + 1)], start=True, stop=True,
                       sem=S["mm"])
            v.wait(S["mm"], mmv)
            dvv = v.op("tensor_copy", dstb_[:, 512 * half:512 * (half + 1)],
                       ps[:, 0:512], sem=S["dv"])
            pool.release(bi, (S["dv"], dvv))
    b1c = sb("b1c", [128, 44], F32)
    for blk in range(44):
        g.dma(b1c[:, blk:blk + 1],
              W["ffn_b1p"][layer, 128 * blk:128 * (blk + 1)].unsqueeze(1), S["swp"])
    sc.wait(S["swp"], S["swp"].n)
    v.wait(S["swp"], S["swp"].n)

    w1s = [sb(f"w1s{i}", [128, 8 * 256]) for i in range(2)]
    w2s = [sb(f"w2s{i}", [128, 1024]) for i in range(2)]
    hbf = sb("hbf", [128, 1024])
    hnb = [sb(f"hnb{i}", [128, 1024]) for i in range(2)]
    hnT = sb("hnT", [128, 2048])
    h2b = sb("h2b", [128, 512])
    gel = sb("gel", [128, 256], F32)
    xzf = sb("xzf", [128, 2048], F32)
    xzb = sb("xzb", [128, 2048])
    xr1 = sb("xr1", [128, 1024], F32)
    xr2 = sb("xr2", [128, 1024], F32)
    tmp = sb("tmp", [128, 1024], F32)
    st1 = sb("st1", [128, 8], F32)
    wld = {}
    hbf_dv = 0
    xr_dv = 0

    for b in range(B):
        for tt in range(T // TTK):
            t0 = TTK * tt

            ser = True

            def vm(meth, *a, **kw):
                val = v.op(meth, *a, sem=S["dv"], **kw)
                if ser:
                    v.wait(S["dv"], val)
                return val

            def sm(meth, *a, **kw):
                val = sc.op(meth, *a, sem=S["ac"], **kw)
                if ser:
                    sc.wait(S["ac"], val)
                return val
            for ck in range(2):
                tc = t0 + 128 * ck
                sy.wait(S["dv"], hbf_dv)
                sy.dma(hbf[:, 0:512], srcb[b, TG + tc:TG + tc + 128, 0:512], S["ld"])
                sy.dma(hbf[:, 512:1024], srcb[b, TG + tc - 1:TG + tc + 127, 512:1024],
                       S["ld"])
                ldv = S["ld"].n
                v.wait(S["ld"], ldv)
                vm("tensor_tensor", tmp[:], hbf[:], hbf[:], ALU.mult)
                vm("tensor_reduce", st1[:, 0:1], hbf[:], AX.X, ALU.add)
                vm("tensor_reduce", st1[:, 1:2], tmp[:], AX.X, ALU.add)
                vm("tensor_scalar_mul", st1[:, 2:3], st1[:, 0:1], 1.0 / 1024)
                vm("tensor_tensor", st1[:, 3:4], st1[:, 2:3], st1[:, 2:3], ALU.mult)
                vm("scalar_tensor_tensor", st1[:, 4:5], st1[:, 1:2], 1.0 / 1024,
                   st1[:, 3:4], ALU.mult, ALU.subtract)
                sc.wait(S["dv"], S["dv"].n)
                sm("activation", st1[:, 4:5], st1[:, 4:5], AF.Ln, bias=1e-5)
                sm("activation", st1[:, 5:6], st1[:, 4:5], AF.Exp, scale=-0.5)
                v.wait(S["ac"], S["ac"].n)
                vm("tensor_tensor", tmp[:], hbf[:],
                   st1[:, 2:3].broadcast_to([128, 1024]), ALU.subtract)
                vm("tensor_tensor", tmp[:], tmp[:],
                   st1[:, 5:6].broadcast_to([128, 1024]), ALU.mult)
                vm("tensor_tensor", tmp[:], tmp[:], bc_lg[:], ALU.mult)
                dv_h = vm("tensor_tensor", hnb[ck][:], tmp[:], bc_lb[:], ALU.add)
                hbf_dv = dv_h
                t.wait(S["dv"], dv_h)
                for bk in range(2):
                    bi, ps = pool.acquire(t)
                    pb16 = ps[:].bitcast(BF16)
                    mmv = None
                    for ci in range(4):
                        cb = 4 * bk + ci
                        for hf in range(2):
                            mmv = t.op("transpose",
                                       pb16[64 * hf:64 * hf + 64,
                                            128 * ci:128 * (ci + 1)],
                                       hnb[ck][:, 128 * cb + 64 * hf:
                                               128 * cb + 64 * (hf + 1)],
                                       id128[:], tile_position=(0, 64 * hf),
                                       sem=(S["mm"] if (ci == 3 and hf == 1)
                                            else None))
                    v.wait(S["mm"], mmv)
                    dvv = vm("tensor_copy",
                             hnT[:].rearrange("p (i c x) -> p i c x", i=8, c=2)
                             [:, 4 * bk:4 * bk + 4, ck, :],
                             pb16[:, 0:512].rearrange("p (i x) -> p i x", i=4))
                    pool.release(bi, (S["dv"], dvv))
            t.wait(S["dv"], S["dv"].n)
            obanks = [pool.acquire(t, hold=True) for _ in range(4)]
            for j in range(NJ):
                pj = j % 2
                g.wait(S["mm"], wld.get(pj, 0))
                g.dma(w1s[pj][:].rearrange("p (i x) -> p i x", i=8),
                      W["ffn_w1p"][layer].rearrange("(i p) o -> p i o", p=128)
                      [:, :, 256 * j:256 * (j + 1)], S["rs8"])
                g.dma(w2s[pj][:], W["ffn_w2p"][layer, 128 * j:128 * (j + 1), :],
                      S["rs8"])
                ldwv = S["rs8"].n
                t.wait(S["rs8"], ldwv)
                biA, psA = pool.acquire(t)
                biG, psG = pool.acquire(t)
                for icb in range(8):
                    t.op("matmul", psA[0:128, 0:256],
                         w1s[pj][:, 256 * icb:256 * icb + 128],
                         hnT[:, 256 * icb:256 * (icb + 1)],
                         start=(icb == 0), stop=(icb == 7))
                for icb in range(8):
                    mmv = t.op("matmul", psG[0:128, 0:256],
                               w1s[pj][:, 256 * icb + 128:256 * icb + 256],
                               hnT[:, 256 * icb:256 * (icb + 1)],
                               start=(icb == 0), stop=(icb == 7),
                               sem=(S["mm"] if icb == 7 else None))
                sc.wait(S["mm"], mmv)
                acv = sm("activation", gel[:], psG[0:128, 0:256], AF.Gelu,
                         bias=b1c[:, 2 * j + 1:2 * j + 2])
                v.wait(S["ac"], acv)
                vm("tensor_tensor", tmp[:, 0:256], psA[0:128, 0:256],
                   b1c[:, 2 * j:2 * j + 1].broadcast_to([128, 256]), ALU.add)
                dv_h2 = vm("tensor_tensor", h2b[:, 0:256], tmp[:, 0:256], gel[:],
                           ALU.mult)
                t.wait(S["dv"], dv_h2)
                pool.release(biA, (S["dv"], dv_h2))
                pool.release(biG, (S["dv"], dv_h2))
                for sub in range(2):
                    for half in range(2):
                        _, pso = obanks[2 * sub + half]
                        mmv = t.op("matmul", pso[0:128, 0:512],
                                   h2b[:, 128 * sub:128 * (sub + 1)],
                                   w2s[pj][:, 512 * half:512 * (half + 1)],
                                   start=(j == 0), stop=(j == NJ - 1),
                                   sem=(S["mm"] if (sub == 1 and half == 1)
                                        else None))
                wld[pj] = S["mm"].n
            sy.wait(S["dv"], xr_dv)
            sy.dma(xr1[:], srcf[b, TG + t0:TG + t0 + 128, :], S["ld"])
            sy.dma(xr2[:], srcf[b, TG + t0 + 128:TG + t0 + 256, :], S["ld"])
            ldv = S["ld"].n
            v.wait(S["ld"], ldv)
            v.wait(S["mm"], S["mm"].n)
            v.wait(S["stv"], S["stv"].n)
            for sub in range(2):
                xsrc = xr1 if sub == 0 else xr2
                for half in range(2):
                    _, pso = obanks[2 * sub + half]
                    vm("tensor_add", xzf[:, 1024 * sub + 512 * half:
                                         1024 * sub + 512 * (half + 1)],
                       pso[0:128, 0:512], xsrc[:, 512 * half:512 * (half + 1)])
            dv_x = v.op("engine_nop", sem=S["dv"])
            for bi, _ in obanks:
                pool.release(bi, (S["dv"], dv_x))
            vm("tensor_tensor", xzf[:, 0:1024], xzf[:, 0:1024], bc_b2[:], ALU.add)
            vm("tensor_tensor", xzf[:, 1024:2048], xzf[:, 1024:2048], bc_b2[:],
               ALU.add)
            dv_x2 = vm("tensor_copy", xzb[:], xzf[:])
            xr_dv = dv_x2
            g.wait(S["dv"], dv_x2)
            for sub in range(2):
                g.dma(dstf[b, TG + t0 + 128 * sub:TG + t0 + 128 * (sub + 1), :],
                      xzf[:, 1024 * sub:1024 * (sub + 1)], S["stv"])
                g.dma(dstb[b, TG + t0 + 128 * sub:TG + t0 + 128 * (sub + 1), :],
                      xzb[:, 1024 * sub:1024 * (sub + 1)], S["stv"])
            g.wait(S["stv"], S["stv"].n)
        barrier(P, S)
    stage.close()


def plan_final(nc, P, S, W, SC, CONST, pool, st, out):
    import contextlib
    stage = contextlib.ExitStack()

    def sb(nm, sh, dt=F32):
        return stage.enter_context(nc.sbuf_tensor("fin_" + nm, sh, dt))

    g, sy, sc, v, t = P.gpsimd, P.sync, P.scalar, P.vector, P.tensor
    xa = [sb(f"xa{p}", [128, 1024]) for p in range(2)]
    xb = [sb(f"xb{p}", [128, 1024]) for p in range(2)]
    xo = [sb(f"xo{p}", [128, 1024]) for p in range(2)]
    tmp = sb("tmp", [128, 1024])
    st1 = sb("st1", [128, 8])
    done_dv = {}
    done_st = {}
    stsem = [S["rs6"], S["rs7"]]
    ldsem = [S["ld"], S["ld2"]]
    for idx, (b, tk) in enumerate([(b, tk) for b in range(B) for tk in range(T // 128)]):
        p = idx % 2
        t0 = 128 * tk

        def vf(meth, *a, **kw):
            val = v.op(meth, *a, sem=S["dv"], **kw)
            v.wait(S["dv"], val)
            return val

        def sf(meth, *a, **kw):
            val = sc.op(meth, *a, sem=S["ac"], **kw)
            sc.wait(S["ac"], val)
            return val
        if idx >= 2:
            sy.wait(S["dv"], done_dv[p])
            sy.wait(stsem[p], done_st[p])
        sy.dma(xa[p][:], SC["x1"][b, TG + t0:TG + t0 + 128, :], ldsem[p])
        sy.dma(xb[p][:], SC["x3"][b, TG + t0:TG + t0 + 128, :], ldsem[p])
        ldv = ldsem[p].n
        v.wait(ldsem[p], ldv)
        vf("tensor_tensor", xo[p][:], xa[p][:], xb[p][:], ALU.add)
        vf("tensor_tensor", tmp[:], xo[p][:], xo[p][:], ALU.mult)
        vf("tensor_reduce", st1[:, 0:1], xo[p][:], AX.X, ALU.add)
        vf("tensor_reduce", st1[:, 1:2], tmp[:], AX.X, ALU.add)
        vf("tensor_scalar_mul", st1[:, 2:3], st1[:, 0:1], 1.0 / 1024)
        vf("tensor_tensor", st1[:, 3:4], st1[:, 2:3], st1[:, 2:3], ALU.mult)
        vf("scalar_tensor_tensor", st1[:, 4:5], st1[:, 1:2], 1.0 / 1024,
           st1[:, 3:4], ALU.mult, ALU.subtract)
        sc.wait(S["dv"], S["dv"].n)
        sf("activation", st1[:, 4:5], st1[:, 4:5], AF.Ln, bias=1e-5)
        sf("activation", st1[:, 5:6], st1[:, 4:5], AF.Exp, scale=-0.5)
        v.wait(S["ac"], S["ac"].n)
        vf("tensor_tensor", xo[p][:], xo[p][:],
           st1[:, 2:3].broadcast_to([128, 1024]), ALU.subtract)
        dvf = vf("tensor_tensor", xo[p][:], xo[p][:],
                 st1[:, 5:6].broadcast_to([128, 1024]), ALU.mult)
        done_dv[p] = dvf
        g.wait(S["dv"], dvf)
        g.dma(out[b, t0:t0 + 128, :], xo[p][:], stsem[p])
        done_st[p] = stsem[p].n
    g.wait(S["rs6"], S["rs6"].n)
    g.wait(S["rs7"], S["rs7"].n)
    barrier(P, S)
    stage.close()


def host_inputs(inputs, core):
    m = {}
    m["x"] = np.ascontiguousarray(inputs["L0_attn_x"][2 * core:2 * core + 2])
    for k, v in inputs.items():
        if k == "L0_attn_x":
            continue
        m[k] = v
    w1 = np.asarray(inputs["L0_ffn_w1"])
    b1 = np.asarray(inputs["L0_ffn_b1"])
    w2 = np.asarray(inputs["L0_ffn_w2"])
    HID, HP = 2730, 2816
    w1p = np.zeros((2, w1.shape[1], 2 * HP), w1.dtype)
    b1p = np.zeros((2, 2 * HP), b1.dtype)
    for blk in range(22):
        lo, hi = 128 * blk, min(128 * (blk + 1), HID)
        w1p[:, :, 256 * blk:256 * blk + (hi - lo)] = w1[:, :, lo:hi]
        w1p[:, :, 256 * blk + 128:256 * blk + 128 + (hi - lo)] = \
            w1[:, :, HID + lo:HID + hi]
        b1p[:, 256 * blk:256 * blk + (hi - lo)] = b1[:, lo:hi]
        b1p[:, 256 * blk + 128:256 * blk + 128 + (hi - lo)] = b1[:, HID + lo:HID + hi]
    w2p = np.zeros((2, HP, w2.shape[2]), w2.dtype)
    w2p[:, :HID] = w2
    m["w1p"] = w1p
    m["b1p"] = b1p
    m["w2p"] = w2p
    del m["L0_ffn_w1"], m["L0_ffn_b1"], m["L0_ffn_w2"]
    half = np.arange(0, 64, 2, dtype=np.float32)
    inv_freq = (1.0 / 10000.0 ** (half / np.float32(64))).astype(np.float32)
    tpos = np.arange(T, dtype=np.float32)
    freqs = (tpos[:, None] * inv_freq[None, :]).astype(np.float32)
    freqs = np.concatenate([freqs, freqs], -1)
    base = ((half + 0.4 * 64) / (1.4 * 64)).astype(np.float64)
    power = (tpos.astype(np.float64) - T // 2) / 256.0
    scale = base[None, :] ** power[:, None]
    scale = np.concatenate([scale, scale], -1)
    cosf = np.cos(freqs, dtype=np.float32)
    sinf = np.sin(freqs, dtype=np.float32)
    scale = scale.astype(np.float32)
    m["rot_qc"] = np.ascontiguousarray(cosf * scale * 8.0).astype(np.float32)
    m["rot_qs"] = np.ascontiguousarray(sinf * scale * 8.0).astype(np.float32)
    m["rot_kc"] = np.ascontiguousarray(cosf / scale).astype(np.float32)
    m["rot_ks"] = np.ascontiguousarray(sinf / scale).astype(np.float32)
    return m


def kernel(**inputs):
    nc = build_program()
    in_maps = [host_inputs(inputs, c) for c in range(8)]
    res = run_bass_kernel_spmd(nc, in_maps, core_ids=list(range(8)))
    return np.concatenate([r["out"] for r in res.results], axis=0)
